# version 09
# baseline (speedup 1.0000x reference)
import contextlib
import numpy as np

import concourse.bass as bass
import concourse.mybir as mybir
from concourse.bass_utils import run_bass_kernel_spmd

F32 = mybir.dt.float32
BF16 = mybir.dt.bfloat16
ALU = mybir.AluOpType
ACTF = mybir.ActivationFunctionType
AX = mybir.AxisListType

D = 1024
DI = 2048
NST = 16
BATCH, SEQ = 2, 8192
NCORES = 8
CPS = 4
TOKS = SEQ // CPS
TT = TOKS + 128
NT = TT // 128
NRT = TOKS // 128
KC = D // 128
CC = DI // 128
NE = 32
MW = DI + NST
EPS = 1e-5
CAP = 256
NSLOT = NE * CAP


class Tile:
    __slots__ = ("name", "writer", "readers")

    def __init__(self, name):
        self.name = name
        self.writer = []
        self.readers = []

    def inherit(self, parents):
        deps = []
        for p in parents:
            deps.extend(p.writer)
            deps.extend(p.readers)
        self.readers = deps
        return self


class Prog:
    ENGS = ("pe", "dve", "act", "gp", "sp")

    def __init__(self, nc):
        self.nc = nc
        self.ths = {e: [] for e in self.ENGS}
        self.ctx = contextlib.ExitStack()
        self.semh = {}
        self.cnt = {}
        self.satisfied = {e: {} for e in self.ENGS}
        self.esem = {e: "E_" + e for e in self.ENGS}

    def sem(self, name):
        if name not in self.semh:
            self.semh[name] = self.ctx.enter_context(self.nc.semaphore(name))
            self.cnt[name] = 0
        return self.semh[name]

    def _emit_wait(self, eng, sname, val):
        if val <= 0:
            return
        if self.satisfied[eng].get(sname, 0) >= val:
            return
        self.satisfied[eng][sname] = val
        h = self.sem(sname)
        self.ths[eng].append(lambda E, h=h, v=val: E.wait_ge(h, v))

    def _deps(self, eng, reads, writes):
        need = {}
        for t in reads:
            for s, v in t.writer:
                need[s] = max(need.get(s, 0), v)
        for t in writes:
            for s, v in t.writer:
                need[s] = max(need.get(s, 0), v)
            for s, v in t.readers:
                need[s] = max(need.get(s, 0), v)
        for s, v in sorted(need.items()):
            self._emit_wait(eng, s, v)

    def op(self, eng, fn, reads=(), writes=(), sem=None):
        self._deps(eng, reads, writes)
        sname = sem or self.esem[eng]
        h = self.sem(sname)
        self.ths[eng].append(lambda E, h=h, fn=fn: fn(E).then_inc(h, 1))
        self.cnt[sname] += 1
        v = self.cnt[sname]
        for t in reads:
            t.readers.append((sname, v))
        for t in writes:
            t.writer = [(sname, v)]
            t.readers = []

    def dma(self, eng, sname, fn, reads=(), writes=()):
        self._deps(eng, reads, writes)
        h = self.sem(sname)
        self.ths[eng].append(lambda E, h=h, fn=fn: fn(E).then_inc(h, 16))
        self.cnt[sname] += 16
        v = self.cnt[sname]
        for t in reads:
            t.readers.append((sname, v))
        for t in writes:
            t.writer = [(sname, v)]
            t.readers = []

    def relay(self, eng, sname):
        v = self.cnt[sname]
        self._emit_wait(eng, sname, v)
        es = self.esem[eng]
        h = self.sem(es)
        self.ths[eng].append(lambda E, h=h: E.nop().then_inc(h, 1))
        self.cnt[es] += 1
        return (es, self.cnt[es])

    def emit(self):
        nc = self.nc
        block = self.ctx.enter_context(nc.Block())
        engobj = {"pe": block.tensor, "dve": block.vector, "act": block.scalar,
                  "gp": block.gpsimd, "sp": block.sync}
        for e in self.ENGS:
            ths = self.ths[e]

            def body(E, ths=ths):
                for t in ths:
                    t(E)
            engobj[e](body)


def build_nc(toks=TOKS):
    nt = toks // 128 + 1
    nrt = toks // 128
    tt = toks + 128

    nc = bass.Bass("TRN2", target_bir_lowering=False, debug=False)
    P = Prog(nc)

    d_x = nc.dram_tensor("x", [tt, D], F32, kind="ExternalInput").ap()
    d_win = nc.dram_tensor("w_inT", [D, MW], F32, kind="ExternalInput").ap()
    d_bin = nc.dram_tensor("b_in_sb", [128, CC + 1], F32, kind="ExternalInput").ap()
    d_wc = nc.dram_tensor("w_conv_sb", [128, CC * 4], F32, kind="ExternalInput").ap()
    d_bc = nc.dram_tensor("b_conv_sb", [128, CC], F32, kind="ExternalInput").ap()
    d_wxdt = nc.dram_tensor("wxdt_sb", [128, CC * 48], F32, kind="ExternalInput").ap()
    d_bdt = nc.dram_tensor("b_dt", [NST, 1], F32, kind="ExternalInput").ap()
    d_wo = nc.dram_tensor("w_out16T", [NST, D], F32, kind="ExternalInput").ap()
    d_rw = nc.dram_tensor("router_sb", [128, KC * NE], F32, kind="ExternalInput").ap()
    d_ew = nc.dram_tensor("expw", [NE, 128, 2, KC * 512], F32, kind="ExternalInput").ap()
    d_halo = nc.dram_tensor("halo_mask", [128, 3], F32, kind="ExternalInput").ap()
    d_mall = nc.dram_tensor("m_all", [NST, NCORES], F32, kind="ExternalInput").ap()
    d_out = nc.dram_tensor("out", [toks, D], F32, kind="ExternalOutput").ap()

    d_ecap1 = nc.dram_tensor("ecap1", [NE, 1], F32, kind="ExternalInput").ap()
    d_ewb = nc.dram_tensor("ewb", [NE, 128, 2, KC * 512], BF16).ap()
    T_ewb = Tile("ewb")
    d_xn2tm = nc.dram_tensor("xn2tm", [toks, D], BF16).ap()
    d_xs = nc.dram_tensor("xs", [NSLOT, D], BF16).ap()
    d_ydr = nc.dram_tensor("ydr", [NSLOT, D], BF16).ap()
    T_xn2tm = Tile("xn2tm")
    T_xs = Tile("xs")
    T_ydr = Tile("ydr")
    d_cci = nc.dram_tensor("cc_in", [2 * NST], F32).ap()
    d_cco = nc.dram_tensor("cc_out", [NCORES, 2 * NST], F32).ap()
    t_cci = Tile("cc_in")
    t_cco = Tile("cc_out")

    A = nc.alloc_sbuf_tensor
    AT = nc.alloc_sbuf_tensor_at

    def addr(h):
        return nc.lookup_mloc(h).addr

    x2_h = A("x2", [128, nrt * D], F32)
    xnfm_h = A("xnfm", [128, KC * tt], BF16)
    win_h = A("win", [128, KC * MW], BF16)
    xzb_h = A("xzb", [128, tt], F32)
    cvb_h = A("cvb", [128, tt], F32)
    xcr_h = A("xcr", [128, 2 * toks], BF16)
    xnb_h = A("xnb", [128, 2 * D], BF16)
    xld_h = A("xld", [128, 4 * D], F32)
    xc16_h = A("xc16", [NST, toks], F32)
    z16_h = A("z16", [NST, tt], F32)
    xdt_h = A("xdt48", [48, toks], F32)
    wm_h = A("wm", [128, nrt * NE], F32)
    identf_h = A("identf", [32, 32], F32)
    ecap_h2 = A("ecapc", [NE, 1], F32)
    wmx_h = A("wmx", [128, 2 * nrt], F32)
    d12f_h = A("d12f", [128, 2 * nrt], F32)
    d1i_h = A("d1i", [128, nrt], mybir.dt.int32)
    d2i_h = A("d2i", [128, nrt], mybir.dt.int32)
    m12b_h = A("m12b", [128, NE], BF16)
    st_h = A("stats", [128, 64], F32)
    bin_h = A("binc", [128, CC + 1], F32)
    wc_h = A("wcc", [128, CC * 4], F32)
    bc_h = A("bcc", [128, CC], F32)
    wxdt_h = A("wxdtc", [128, CC * 48], BF16)
    bdt_h = A("bdtc", [NST, 1], F32)
    wo_h = A("woc", [NST, D], BF16)
    rw_h = A("rwc", [128, KC * NE], BF16)
    halo_h = A("haloc", [128, 3], F32)
    mall_h = A("mallc", [NST, NCORES], F32)
    ident_h = A("identc", [128, 128], BF16)
    sm_h = A("scansm", [NST, 32], F32)
    pa_h = A("pa_all", [NST, NCORES], F32)
    he_h = A("he_all", [NST, NCORES], F32)
    pam_h = A("pam", [NST, NCORES], F32)
    hem_h = A("hem", [NST, NCORES], F32)
    st8_h = A("st8", [NST, NCORES], F32)

    if toks * 4 >= D * 4:
        sqd_h = AT("sqd", [128, D], F32, offset=addr(xdt_h))
    else:
        sqd_h = A("sqd", [128, D], F32)
    t_h_h = AT("t_h", [NST, toks], F32, offset=addr(xnfm_h))
    y16b_h = AT("y16b", [NST, toks], BF16, offset=addr(xnfm_h) + toks * 4)
    xn2fm_h = AT("xn2fm", [128, KC * toks], BF16, offset=addr(xnfm_h))
    we_hs = [AT(f"we{q}", [128, KC * 512], BF16,
                offset=addr(win_h) + q * KC * 512 * 2) for q in range(4)]
    assert 4 * KC * 512 * 2 <= KC * MW * 2
    txb_h = AT("t_xdblB", [NST, toks], F32, offset=addr(xzb_h))
    tdl_h = AT("t_delta", [NST, toks], F32, offset=addr(cvb_h))
    xzb2_h = AT("xzb2", [128, tt], F32, offset=addr(x2_h))
    cvb2_h = AT("cvb2", [128, tt], F32, offset=addr(x2_h) + tt * 4)
    ttm_h = AT("t_tmp", [NST, toks], F32, offset=addr(xcr_h))

    m12t_h = AT("m12t", [NE, toks], F32, offset=addr(xzb_h))
    sid_h = AT("sid", [NE, toks], F32, offset=addr(z16_h))
    m1a_h = AT("m1all", [128, nrt * NE], F32, offset=addr(cvb_h))
    m2a_h = AT("m2all", [128, nrt * NE], F32,
               offset=addr(cvb_h) + nrt * NE * 4)
    assert 2 * nrt * NE * 4 <= tt * 4
    if nrt * NE * 4 >= D * 2:
        zf_h = AT("zfill", [128, D], BF16, offset=addr(wm_h))
    else:
        zf_h = A("zfill", [128, D], BF16)
    yst_h = AT("ystage", [128, 2 * 2 * D], BF16, offset=addr(xld_h))
    wreg = addr(xnfm_h)
    wsz = KC * tt * 2
    woff = [0]

    def at_w(name, shape, dtype):
        nbytes = 1
        for s_ in shape[1:]:
            nbytes *= s_
        nbytes *= (2 if dtype == BF16 else 4)
        if woff[0] + nbytes <= wsz:
            h = AT(name, shape, dtype, offset=wreg + woff[0])
            woff[0] += nbytes
        else:
            h = A(name, shape, dtype)
        return h
    xssb_h = at_w("xssb", [128, 2 * 2 * D], BF16)
    xsfm_h = at_w("xsfm", [128, 2 * 2 * D], BF16)
    gbuf_h = at_w("gbuf", [128, 2 * D], BF16)

    rem = nc.sbuf_bytes_remaining
    assert rem >= 0, f"SBUF over budget: {rem}"

    ps = [nc.alloc_psum_tensor(f"ps{i}", [128, 512], F32).ap() for i in range(8)]

    x2 = x2_h.ap()
    xnfm = xnfm_h.ap()
    win = win_h.ap()
    xzb = xzb_h.ap()
    cvb = cvb_h.ap()
    xcr = xcr_h.ap()
    xnb = xnb_h.ap()
    xld = xld_h.ap()
    xc16 = xc16_h.ap()
    z16 = z16_h.ap()
    xdt = xdt_h.ap()
    wm = wm_h.ap()
    sqd = sqd_h.ap()
    stats = st_h.ap()
    binc = bin_h.ap()
    wcc = wc_h.ap()
    bcc = bc_h.ap()
    wxdtc = wxdt_h.ap()
    bdtc = bdt_h.ap()
    woc = wo_h.ap()
    rwc = rw_h.ap()
    haloc = halo_h.ap()
    mallc = mall_h.ap()
    ident = ident_h.ap()
    sm = sm_h.ap()
    pa_all = pa_h.ap()
    he_all = he_h.ap()
    pam = pam_h.ap()
    hem = hem_h.ap()
    st8 = st8_h.ap()
    identf = identf_h.ap()
    ecapc = ecap_h2.ap()
    wmx = wmx_h.ap()
    d12f = d12f_h.ap()
    d1i = d1i_h.ap()
    d2i = d2i_h.ap()
    m12b = m12b_h.ap()
    m12t = m12t_h.ap()
    sid = sid_h.ap()
    m1a = m1a_h.ap()
    m2a = m2a_h.ap()
    zf = zf_h.ap()
    yst = yst_h.ap()
    xssb = xssb_h.ap()
    xsfm = xsfm_h.ap()
    gbuf = gbuf_h.ap()
    t_h = t_h_h.ap()
    y16b = y16b_h.ap()
    xn2fm = xn2fm_h.ap()
    we = [h.ap() for h in we_hs]
    txb = txb_h.ap()
    tdl = tdl_h.ap()
    ttm = ttm_h.ap()
    xzbs = [xzb, xzb2_h.ap()]
    cvbs = [cvb, cvb2_h.ap()]

    T_x2 = [Tile(f"x2_{i}") for i in range(nrt)]
    T_xnfm = Tile("xnfm")
    T_win = Tile("win")
    T_xzbs = [Tile("xzb0"), Tile("xzb1")]
    T_cvbs = [Tile("cvb0"), Tile("cvb1")]
    T_xcr = [Tile("xcr0"), Tile("xcr1")]
    T_xnb = [Tile("xnb0"), Tile("xnb1")]
    T_xld = [Tile(f"xld{i}") for i in range(4)]
    T_xc16 = Tile("xc16")
    T_z16 = Tile("z16")
    T_xdt = Tile("xdt")
    T_sidtm = Tile("sidtm")
    T_sqd = Tile("sqd")
    T_stats = [Tile("stats0"), Tile("stats1")]
    T_consts = Tile("consts")
    T_ident = Tile("ident")
    T_ps = [Tile(f"ps{i}") for i in range(8)]
    T_smalls = Tile("scansmalls")
    T_pa = Tile("pa_all")
    T_he = Tile("he_all")
    T_pam = Tile("pam")
    T_hem = Tile("hem")
    T_st8 = Tile("st8")

    P.op("gp", lambda E: E.memset(ident, 0.0), writes=[T_ident])
    P.op("gp", lambda E: E.affine_select(
        out=ident, in_=ident, compare_op=ALU.not_equal, fill=1.0, base=0,
        pattern=[[-1, 128]], channel_multiplier=1),
        reads=[T_ident], writes=[T_ident])

    T_identf = Tile("identf")
    P.op("gp", lambda E: E.memset(identf, 0.0), writes=[T_identf])
    P.op("gp", lambda E: E.affine_select(
        out=identf, in_=identf, compare_op=ALU.not_equal, fill=1.0, base=0,
        pattern=[[-1, 32]], channel_multiplier=1),
        reads=[T_identf], writes=[T_identf])

    d_wci = nc.dram_tensor("wm_cci", [NST * 2], F32).ap()
    d_wco = nc.dram_tensor("wm_cco", [NCORES, NST * 2], F32).ap()
    t_wci = Tile("wm_cci")
    P.dma("gp", "wrm", lambda E: E.dma_start(out=d_wci, in_=d_mall[:, 0:2]),
          writes=[t_wci])
    P.op("gp", lambda E: E.collective_compute(
        "AllGather", ALU.bypass, replica_groups=[list(range(NCORES))],
        ins=[d_wci.opt()], outs=[d_wco.opt()]),
        reads=[t_wci], writes=[Tile("wm_cco")], sem="ccw")

    P.dma("gp", "G0", lambda E: E.dma_start(
        out=win.rearrange("p (c m) -> p c m", c=KC),
        in_=d_win.rearrange("(c p) m -> p c m", p=128)))
    P.dma("gp", "G0", lambda E: E.dma_start(out=wxdtc, in_=d_wxdt))
    P.dma("gp", "G0", lambda E: E.dma_start(out=woc, in_=d_wo))
    P.dma("gp", "G0", lambda E: E.dma_start(out=rwc, in_=d_rw))
    gdep = P.relay("gp", "G0")
    T_win.writer = [gdep]
    T_bf16c = Tile("bf16consts")
    T_bf16c.writer = [gdep]

    for e in range(NE):
        for h in range(2):
            P.dma("gp", "ewc", lambda E, e=e, h=h: E.dma_start(
                out=d_ewb[e, :, h, :], in_=d_ew[e, :, h, :]))
    T_ewb.writer = [("ewc", P.cnt["ewc"])]

    inv_d = 1.0 / D

    def ln_stats(x_ap, T_xsrc, g, gi, T_stat):
        col = g * 32
        P.op("dve", lambda E: E.reduce_sum(stats[:, col + gi:col + gi + 1],
                                           x_ap, axis=AX.X),
             reads=[T_xsrc], writes=[T_stat])
        P.op("act", lambda E: E.activation(out=sqd, in_=x_ap, func=ACTF.Square,
                                           accum_out=stats[:, col + 4 + gi:col + 5 + gi]),
             reads=[T_xsrc], writes=[T_sqd, T_stat])

    def ln_chain(g, n, T_stat):
        col = g * 32
        ssum = stats[:, col:col + n]
        ss2 = stats[:, col + 4:col + 4 + n]
        mean = stats[:, col + 8:col + 8 + n]
        ex2 = stats[:, col + 12:col + 12 + n]
        var = stats[:, col + 16:col + 16 + n]
        rstd = stats[:, col + 20:col + 20 + n]
        nbias = stats[:, col + 24:col + 24 + n]
        P.op("dve", lambda E: E.tensor_scalar(out=mean, in0=ssum, scalar1=inv_d,
                                              scalar2=None, op0=ALU.mult),
             reads=[T_stat], writes=[T_stat])
        P.op("dve", lambda E: E.tensor_scalar(out=ex2, in0=ss2, scalar1=inv_d,
                                              scalar2=None, op0=ALU.mult),
             reads=[T_stat], writes=[T_stat])
        P.op("dve", lambda E: E.tensor_tensor(out=var, in0=mean, in1=mean,
                                              op=ALU.mult),
             reads=[T_stat], writes=[T_stat])
        P.op("dve", lambda E: E.tensor_sub(var, var, ex2),
             reads=[T_stat], writes=[T_stat])
        P.op("dve", lambda E: E.tensor_scalar(out=var, in0=var, scalar1=EPS,
                                              scalar2=-1.0, op0=ALU.subtract,
                                              op1=ALU.mult),
             reads=[T_stat], writes=[T_stat])
        P.op("act", lambda E: E.activation(out=var, in_=var, func=ACTF.Sqrt),
             reads=[T_stat], writes=[T_stat])
        P.op("dve", lambda E: E.reciprocal(rstd, var),
             reads=[T_stat], writes=[T_stat])
        P.op("dve", lambda E: E.tensor_scalar(out=nbias, in0=mean, scalar1=-1.0,
                                              scalar2=None, op0=ALU.mult),
             reads=[T_stat], writes=[T_stat])
        P.op("dve", lambda E: E.tensor_mul(nbias, nbias, rstd),
             reads=[T_stat], writes=[T_stat])

    def ln_norm(x_ap, T_xsrc, g, gi, T_stat, xn_ap, T_xn):
        col = g * 32
        rstd = stats[:, col + 20 + gi:col + 21 + gi]
        nbias = stats[:, col + 24 + gi:col + 25 + gi]
        P.op("act", lambda E: E.activation(out=xn_ap, in_=x_ap,
                                           func=ACTF.Identity,
                                           scale=rstd, bias=nbias),
             reads=[T_xsrc, T_stat], writes=[T_xn])

    def transpose_tile(xn_ap, T_xn, dst_base_col, dst_stride, T_dst, dst_ap_fn):
        for half in range(2):
            pst = T_ps[half]
            psa = ps[half]

            def mmt(E, half=half, xn_ap=xn_ap, psa=psa):
                last = None
                for q in range(4):
                    c = half * 4 + q
                    last = E.matmul(psa[:, q * 128:(q + 1) * 128],
                                    xn_ap[:, c * 128:(c + 1) * 128],
                                    ident, start=True, stop=True)
                return last
            P.op("pe", mmt, reads=[T_xn, T_ident], writes=[pst])
            if half == 0:
                P.op("dve", lambda E, half=half, psa=psa: E.tensor_copy(
                    dst_ap_fn(half), psa.rearrange("p (c t) -> p c t", c=4)),
                    reads=[pst], writes=[T_dst])
            else:
                P.op("act", lambda E, half=half, psa=psa: E.activation(
                    out=dst_ap_fn(half),
                    in_=psa.rearrange("p (c t) -> p c t", c=4),
                    func=ACTF.Copy),
                    reads=[pst], writes=[T_dst])

    for g0 in range(0, nt, 4):
        gn = min(4, nt - g0)
        g = (g0 // 4) % 2
        for gi in range(gn):
            tau = g0 + gi
            r = tau % 4
            x_ap = xld[:, r * D:(r + 1) * D]
            P.dma("sp", f"xr{r}", lambda E, x_ap=x_ap, tau=tau: E.dma_start(
                out=x_ap, in_=d_x[tau * 128:(tau + 1) * 128, :]),
                writes=[T_xld[r]])
            ln_stats(x_ap, T_xld[r], g, gi, T_stats[g])
        ln_chain(g, gn, T_stats[g])
        for gi in range(gn):
            tau = g0 + gi
            r = tau % 4
            s = tau % 2
            x_ap = xld[:, r * D:(r + 1) * D]
            xn_ap = xnb[:, s * D:(s + 1) * D]
            ln_norm(x_ap, T_xld[r], g, gi, T_stats[g], xn_ap, T_xnb[s])

            def dstfn(half, tau=tau):
                return xnfm.rearrange("p (c t) -> p c t", c=KC)[
                    :, half * 4:half * 4 + 4, tau * 128:(tau + 1) * 128]
            transpose_tile(xn_ap, T_xnb[s], None, None, T_xnfm, dstfn)

    T_zf = Tile("zfill")
    P.op("dve", lambda E: E.memset(zf, 0.0), writes=[T_zf])
    for j in range(NSLOT // 128):
        P.dma("sp", "zf", lambda E, j=j: E.dma_start(
            out=d_xs[j * 128:(j + 1) * 128, :], in_=zf),
            reads=[T_zf])

    for dst, srcp in ((binc, d_bin), (wcc, d_wc), (bcc, d_bc), (bdtc, d_bdt),
                      (haloc, d_halo), (mallc, d_mall), (ecapc, d_ecap1)):
        P.dma("sp", "L0", lambda E, dst=dst, srcp=srcp: E.dma_start(out=dst, in_=srcp))
    P.relay("sp", "L0")
    T_consts.writer = [("E_sp", P.cnt["E_sp"])]

    tqs = []
    c0 = 0
    while c0 < tt:
        w = min(512, tt - c0)
        tqs.append((c0, w))
        c0 += w

    pend = None

    def emit_sil_xp(pend):
        mc, xc_ap, cv_ap, T_cv, s = pend
        P.op("act", lambda E, mc=mc, xc_ap=xc_ap, cv_ap=cv_ap: E.activation(
            out=xc_ap, in_=cv_ap[:, 128:tt], func=ACTF.Silu,
            bias=bcc[:, mc:mc + 1], scale=1.0),
            reads=[T_cv, T_consts], writes=[T_xcr[s]])
        if mc == 0:
            P.op("act", lambda E, cv_ap=cv_ap: E.activation(
                out=xc16, in_=cv_ap[0:NST, 128:tt], func=ACTF.Silu,
                bias=bcc[0:NST, 0:1], scale=1.0),
                reads=[T_cv, T_consts], writes=[T_xc16])

        def mmx(E, mc=mc, xc_ap=xc_ap):
            last = None
            for q in range(toks // 512):
                last = E.matmul(
                    ps[4 + q][0:48, :],
                    wxdtc.rearrange("p (c m) -> p c m", c=CC)[:, mc, :],
                    xc_ap[:, q * 512:(q + 1) * 512],
                    start=(mc == 0), stop=(mc == CC - 1),
                    skip_group_check=True)
            return last
        P.op("pe", mmx, reads=[T_bf16c, T_xcr[s]],
             writes=[T_ps[4 + q] for q in range(toks // 512)])

    for mc in range(CC + 1):
        is_z = mc == CC
        mwid = NST if is_z else 128
        moff = mc * 128
        s = mc % 2
        xz_ap = xzbs[s]
        cv_ap = cvbs[s]
        T_xz = T_xzbs[s]
        T_cv = T_cvbs[s]
        for qi, (t0, w) in enumerate(tqs):
            bank = 2 + (qi % 2)
            pst = T_ps[bank]
            psa = ps[bank]

            def mmw(E, mc=mc, mwid=mwid, moff=moff, t0=t0, w=w, psa=psa):
                last = None
                for ck in range(KC):
                    last = E.matmul(
                        psa[0:mwid, 0:w],
                        win.rearrange("p (c m) -> p c m", c=KC)[:, ck, moff:moff + mwid],
                        xnfm[:, ck * tt + t0: ck * tt + t0 + w],
                        start=(ck == 0), stop=(ck == KC - 1))
                return last
            P.op("pe", mmw, reads=[T_win, T_xnfm], writes=[pst])
            if is_z:
                P.op("act", lambda E, t0=t0, w=w, psa=psa: E.activation(
                    out=z16[:, t0:t0 + w], in_=psa[0:NST, 0:w],
                    func=ACTF.Identity,
                    bias=binc[0:NST, CC:CC + 1], scale=1.0),
                    reads=[pst, T_consts], writes=[T_z16])
            else:
                P.op("act", lambda E, t0=t0, w=w, psa=psa, mc=mc, xz_ap=xz_ap:
                     E.activation(
                         out=xz_ap[:, t0:t0 + w], in_=psa[:, 0:w],
                         func=ACTF.Identity,
                         bias=binc[:, mc:mc + 1], scale=1.0),
                     reads=[pst, T_consts], writes=[T_xz])
        if is_z:
            continue
        P.op("dve", lambda E, xz_ap=xz_ap: E.tensor_mul(
            xz_ap[:, 125:128], xz_ap[:, 125:128], haloc),
            reads=[T_xz, T_consts], writes=[T_xz])
        P.op("act", lambda E, mc=mc, xz_ap=xz_ap, cv_ap=cv_ap: E.activation(
            out=cv_ap[:, 125:tt], in_=xz_ap[:, 122:tt - 3], func=ACTF.Identity,
            scale=wcc[:, mc * 4:mc * 4 + 1], bias=0.0),
            reads=[T_xz, T_consts], writes=[T_cv])
        for k in (1, 2, 3):
            P.op("dve", lambda E, mc=mc, k=k, xz_ap=xz_ap, cv_ap=cv_ap:
                 E.scalar_tensor_tensor(
                     out=cv_ap[:, 125:tt], in0=xz_ap[:, 122 + k:tt - 3 + k],
                     scalar=wcc[:, mc * 4 + k:mc * 4 + k + 1],
                     in1=cv_ap[:, 125:tt],
                     op0=ALU.mult, op1=ALU.add),
                 reads=[T_xz, T_cv, T_consts], writes=[T_cv])
        if pend is not None:
            emit_sil_xp(pend)
        pend = (mc, xcr[:, s * toks:(s + 1) * toks], cv_ap, T_cv, s)
    emit_sil_xp(pend)

    nq = toks // 512
    for q in range(nq):
        P.op("dve", lambda E, q=q: E.tensor_copy(xdt[:, q * 512:(q + 1) * 512],
                                                 ps[4 + q][0:48, :]),
             reads=[T_ps[4 + q]], writes=[T_xdt])
    T_txb = Tile("t_xdblB").inherit([T_xzbs[0]])
    T_tdl = Tile("t_delta").inherit([T_cvbs[0]])
    T_ttm = Tile("t_tmp").inherit(T_xcr)
    P.dma("sp", "sh1", lambda E: E.dma_start(out=txb, in_=xdt[16:32, :]),
          reads=[T_xdt], writes=[T_txb])
    P.dma("sp", "sh2", lambda E: E.dma_start(out=tdl, in_=xdt[32:48, :]),
          reads=[T_xdt], writes=[T_tdl])
    Aap = xdt[0:NST, :]
    P.op("act", lambda E: E.activation(out=Aap, in_=Aap, func=ACTF.Abs),
         reads=[T_xdt], writes=[T_xdt])
    P.op("act", lambda E: E.activation(out=ttm, in_=tdl, func=ACTF.Abs,
                                       bias=bdtc[:, 0:1], scale=1.0),
         reads=[T_tdl, T_consts], writes=[T_ttm])
    P.op("act", lambda E: E.activation(out=ttm, in_=ttm, func=ACTF.Exp,
                                       scale=-1.0),
         reads=[T_ttm], writes=[T_ttm])
    P.op("act", lambda E: E.activation(out=ttm, in_=ttm, func=ACTF.Ln,
                                       bias=1.0, scale=1.0),
         reads=[T_ttm], writes=[T_ttm])
    P.op("act", lambda E: E.activation(out=tdl, in_=tdl, func=ACTF.Relu,
                                       bias=bdtc[:, 0:1], scale=1.0),
         reads=[T_tdl, T_consts], writes=[T_tdl])
    P.op("dve", lambda E: E.tensor_add(tdl, tdl, ttm),
         reads=[T_tdl, T_ttm], writes=[T_tdl])
    P.op("dve", lambda E: E.tensor_mul(Aap, Aap, tdl),
         reads=[T_xdt, T_tdl], writes=[T_xdt])
    P.op("dve", lambda E: E.reduce_sum(sm[:, 0:1], Aap, axis=AX.X),
         reads=[T_xdt], writes=[T_smalls])
    P.op("act", lambda E: E.activation(out=sm[:, 0:1], in_=sm[:, 0:1],
                                       func=ACTF.Exp, scale=-1.0),
         reads=[T_smalls], writes=[T_smalls])
    P.op("act", lambda E: E.activation(out=Aap, in_=Aap, func=ACTF.Exp,
                                       scale=-1.0),
         reads=[T_xdt], writes=[T_xdt])
    P.op("dve", lambda E: E.tensor_mul(txb, txb, tdl),
         reads=[T_txb, T_tdl], writes=[T_txb])
    P.op("dve", lambda E: E.tensor_mul(txb, txb, xc16),
         reads=[T_txb, T_xc16], writes=[T_txb])
    T_th = Tile("t_h").inherit([T_xnfm])
    P.op("dve", lambda E: E.tensor_tensor_scan(out=t_h, data0=Aap, data1=txb,
                                               initial=0.0, op0=ALU.mult,
                                               op1=ALU.add),
         reads=[T_xdt, T_txb], writes=[T_th])
    P.dma("sp", "cci", lambda E: E.dma_start(out=d_cci[0:NST], in_=sm[:, 0:1]),
          reads=[T_smalls], writes=[t_cci])
    P.dma("sp", "cci", lambda E: E.dma_start(out=d_cci[NST:2 * NST],
                                             in_=t_h[:, toks - 1:toks]),
          reads=[T_th], writes=[t_cci])
    P.op("gp", lambda E: E.collective_compute(
        "AllGather", ALU.bypass, replica_groups=[list(range(NCORES))],
        ins=[d_cci.opt()], outs=[d_cco.opt()]),
        reads=[t_cci], writes=[t_cco], sem="cc")
    T_we = [Tile(f"we{q}").inherit([T_win]) for q in range(4)]
    for q in range(4):
        P.dma("sp", f"we{q}", lambda E, q=q: E.dma_start(
            out=we[q], in_=d_ewb[q // 2, :, q % 2, :]),
            reads=[T_ewb], writes=[T_we[q]])
    P.op("dve", lambda E: E.memset(xc16, 0.0),
         reads=[T_xc16], writes=[T_xc16])
    P.op("dve", lambda E: E.tensor_tensor_scan(out=ttm, data0=Aap, data1=xc16,
                                               initial=1.0, op0=ALU.mult,
                                               op1=ALU.add),
         reads=[T_xdt, T_xc16, T_ttm], writes=[T_ttm])
    P.op("dve", lambda E: E.tensor_mul(t_h, t_h, z16[:, 128:tt]),
         reads=[T_th, T_z16], writes=[T_th])
    P.op("dve", lambda E: E.tensor_mul(ttm, ttm, z16[:, 128:tt]),
         reads=[T_ttm, T_z16], writes=[T_ttm])
    P.dma("sp", "ccl", lambda E: E.dma_start(
        out=pa_all, in_=d_cco[:, 0:NST].rearrange("c s -> s c"),
        allow_slow_non_contiguous=True),
        reads=[t_cco], writes=[T_pa])
    P.dma("sp", "ccl2", lambda E: E.dma_start(
        out=he_all, in_=d_cco[:, NST:2 * NST].rearrange("c s -> s c"),
        allow_slow_non_contiguous=True),
        reads=[t_cco], writes=[T_he])
    P.op("dve", lambda E: E.tensor_scalar(out=pam, in0=pa_all, scalar1=1.0,
                                          scalar2=None, op0=ALU.subtract),
         reads=[T_pa], writes=[T_pam])
    P.op("dve", lambda E: E.tensor_mul(pam, pam, mallc),
         reads=[T_pam, T_consts], writes=[T_pam])
    P.op("dve", lambda E: E.tensor_scalar(out=pam, in0=pam, scalar1=1.0,
                                          scalar2=None, op0=ALU.add),
         reads=[T_pam], writes=[T_pam])
    P.op("dve", lambda E: E.tensor_mul(hem, he_all, mallc),
         reads=[T_he, T_consts], writes=[T_hem])
    P.op("dve", lambda E: E.tensor_tensor_scan(out=st8, data0=pam, data1=hem,
                                               initial=0.0, op0=ALU.mult,
                                               op1=ALU.add),
         reads=[T_pam, T_hem], writes=[T_st8])
    T_y16b = Tile("y16b").inherit([T_xnfm])
    P.op("dve", lambda E: E.scalar_tensor_tensor(
        out=t_h, in0=ttm, scalar=st8[:, NCORES - 1:NCORES], in1=t_h,
        op0=ALU.mult, op1=ALU.add),
        reads=[T_ttm, T_st8, T_th], writes=[T_th])
    P.op("dve", lambda E: E.tensor_copy(y16b, t_h),
         reads=[T_th], writes=[T_y16b])

    for i in range(nrt):
        if i * D * 4 < 2 * tt * 4:
            T_x2[i].inherit([T_xzbs[1], T_cvbs[1]])
        tau = i + 1
        r = tau % 4
        x_ap = xld[:, r * D:(r + 1) * D]
        P.dma("sp", f"xr{r}", lambda E, x_ap=x_ap, tau=tau: E.dma_start(
            out=x_ap, in_=d_x[tau * 128:(tau + 1) * 128, :]),
            writes=[T_xld[r]])

        def mmo(E, i=i):
            E.matmul(ps[2], y16b[:, i * 128:(i + 1) * 128], woc[:, 0:512],
                     start=True, stop=True)
            return E.matmul(ps[3], y16b[:, i * 128:(i + 1) * 128],
                            woc[:, 512:1024], start=True, stop=True)
        P.op("pe", mmo, reads=[T_y16b, T_bf16c], writes=[T_ps[2], T_ps[3]])
        for half in range(2):
            P.op("dve", lambda E, i=i, half=half, x_ap=x_ap: E.tensor_add(
                x2[:, i * D + half * 512: i * D + (half + 1) * 512],
                x_ap[:, half * 512:(half + 1) * 512],
                ps[2 + half][:, 0:512]),
                reads=[T_xld[r], T_ps[2 + half]], writes=[T_x2[i]])

    T_xn2 = [Tile(f"xn2fm{i}").inherit([T_th, T_y16b, T_xnfm]) for i in range(nrt)]
    for g0 in range(0, nrt, 4):
        gn = min(4, nrt - g0)
        g = (g0 // 4) % 2
        for gi in range(gn):
            i = g0 + gi
            x_ap = x2[:, i * D:(i + 1) * D]
            ln_stats(x_ap, T_x2[i], g, gi, T_stats[g])
        ln_chain(g, gn, T_stats[g])
        for gi in range(gn):
            i = g0 + gi
            s = i % 2
            x_ap = x2[:, i * D:(i + 1) * D]
            xn_ap = xnb[:, s * D:(s + 1) * D]
            ln_norm(x_ap, T_x2[i], g, gi, T_stats[g], xn_ap, T_xnb[s])
            P.dma("sp", f"xnt{s}", lambda E, i=i, xn_ap=xn_ap: E.dma_start(
                out=d_xn2tm[i * 128:(i + 1) * 128, :], in_=xn_ap),
                reads=[T_xnb[s]])

            def dstfn2(half, i=i):
                return xn2fm.rearrange("p (c t) -> p c t", c=KC)[
                    :, half * 4:half * 4 + 4, i * 128:(i + 1) * 128]
            transpose_tile(xn_ap, T_xnb[s], None, None, T_xn2[i], dstfn2)

    IOA = bass.IndirectOffsetOnAxis
    T_m12t = Tile("m12t").inherit([T_txb])
    T_sid = Tile("sid").inherit([T_z16])
    T_m1a = Tile("m1all").inherit([T_tdl])
    T_m2a = Tile("m2all").inherit([T_tdl])
    T_wmx = Tile("wmx")
    T_m12b = Tile("m12b")
    T_d12f = Tile("d12f")
    T_dei = [Tile("d1i"), Tile("d2i")]
    for i in range(nrt):
        s = i % 2

        def mmr(E, i=i):
            last = None
            for ck in range(KC):
                last = E.matmul(ps[2][:, 0:NE],
                                xn2fm[:, ck * toks + i * 128: ck * toks + (i + 1) * 128],
                                rwc[:, ck * NE:(ck + 1) * NE],
                                start=(ck == 0), stop=(ck == KC - 1))
            return last
        P.op("pe", mmr, reads=[T_xn2[i], T_bf16c], writes=[T_ps[2]])
        col = 56 + s * 4
        nmax = stats[:, col:col + 1]
        ssum = stats[:, col + 1:col + 2]
        rw = sqd[:, 8:8 + NE]
        P.op("dve", lambda E, nmax=nmax: E.reduce_max(nmax, ps[2][:, 0:NE],
                                                      axis=AX.X, negate=True),
             reads=[T_ps[2]], writes=[T_stats[s]])
        P.op("act", lambda E, rw=rw, nmax=nmax: E.activation(
            out=rw, in_=ps[2][:, 0:NE], func=ACTF.Exp, bias=nmax, scale=1.0),
            reads=[T_ps[2], T_stats[s]], writes=[T_sqd])
        P.op("dve", lambda E, rw=rw, ssum=ssum: E.reduce_sum(ssum, rw, axis=AX.X),
             reads=[T_sqd], writes=[T_stats[s]])
        P.op("dve", lambda E, nmax=nmax, ssum=ssum: E.reciprocal(nmax, ssum),
             reads=[T_stats[s]], writes=[T_stats[s]])
        P.op("dve", lambda E, rw=rw, nmax=nmax: E.tensor_scalar(
            out=rw, in0=rw, scalar1=nmax, scalar2=None, op0=ALU.mult),
            reads=[T_sqd, T_stats[s]], writes=[T_sqd])
        P.op("dve", lambda E, rw=rw: E.max(out=sqd[:, 0:8], in_=rw),
             reads=[T_sqd], writes=[T_sqd])
        P.op("dve", lambda E, i=i: E.tensor_copy(wmx[:, 2 * i:2 * i + 2],
                                                 sqd[:, 0:2]),
             reads=[T_sqd], writes=[T_wmx])
        P.op("dve", lambda E, rw=rw, i=i: E.tensor_scalar(
            out=m1a[:, i * NE:(i + 1) * NE], in0=rw, scalar1=sqd[:, 0:1],
            scalar2=None, op0=ALU.is_ge),
            reads=[T_sqd], writes=[T_m1a])
        P.op("dve", lambda E, rw=rw: E.tensor_scalar(
            out=rw, in0=rw, scalar1=sqd[:, 1:2], scalar2=None, op0=ALU.is_ge),
            reads=[T_sqd], writes=[T_sqd])
        P.op("dve", lambda E, rw=rw, i=i: E.tensor_sub(
            m2a[:, i * NE:(i + 1) * NE], rw, m1a[:, i * NE:(i + 1) * NE]),
            reads=[T_sqd, T_m1a], writes=[T_m2a])
        P.op("dve", lambda E, rw=rw: E.tensor_copy(m12b, rw),
             reads=[T_sqd], writes=[T_m12b])
        q = i % 4
        P.op("pe", lambda E, q=q: E.matmul(
            ps[3][0:NE, q * 128:(q + 1) * 128], m12b, ident,
            start=True, stop=True),
            reads=[T_m12b, T_ident], writes=[T_ps[3]])
        if q == 3 or i == nrt - 1:
            i0 = i - q
            P.op("dve", lambda E, i0=i0, q=q: E.tensor_copy(
                m12t[:, i0 * 128:(i0 + q + 1) * 128],
                ps[3][0:NE, 0:(q + 1) * 128]),
                reads=[T_ps[3]], writes=[T_m12t])

    P.op("dve", lambda E: E.tensor_tensor_scan(
        out=sid, data0=m12t, data1=m12t, initial=0.0,
        op0=ALU.add, op1=ALU.max),
        reads=[T_m12t], writes=[T_sid])
    P.op("dve", lambda E: E.tensor_scalar_min(sid, sid, float(CAP)),
         reads=[T_sid], writes=[T_sid])
    P.op("dve", lambda E: E.tensor_scalar(
        out=sid, in0=sid, scalar1=ecapc[:, 0:1], scalar2=None, op0=ALU.add),
        reads=[T_sid, T_consts], writes=[T_sid])
    for i in range(nrt):
        q = i % 4
        P.op("pe", lambda E, i=i, q=q: E.transpose(
            ps[3][:, q * NE:(q + 1) * NE], sid[:, i * 128:(i + 1) * 128],
            identf),
            reads=[T_sid, T_identf], writes=[T_ps[3]])
        if q == 3 or i == nrt - 1:
            i0 = i - q
            P.op("dve", lambda E, i0=i0, q=q: E.tensor_copy(
                wm[:, i0 * NE:(i0 + q + 1) * NE], ps[3][:, 0:(q + 1) * NE]),
                reads=[T_ps[3]], writes=[T_sidtm])
    for i in range(nrt):
        stm = wm[:, i * NE:(i + 1) * NE]
        tmp = sqd[:, 8:8 + NE]
        for r, ma, Tma in ((0, m1a, T_m1a), (1, m2a, T_m2a)):
            P.op("dve", lambda E, stm=stm, tmp=tmp, ma=ma, i=i: E.tensor_mul(
                tmp, stm, ma[:, i * NE:(i + 1) * NE]),
                reads=[T_sidtm, Tma], writes=[T_sqd])
            P.op("dve", lambda E, tmp=tmp, i=i, r=r: E.reduce_sum(
                d12f[:, r * nrt + i:r * nrt + i + 1], tmp, axis=AX.X),
                reads=[T_sqd], writes=[T_d12f])
    P.op("dve", lambda E: E.tensor_copy(d1i, d12f[:, 0:nrt]),
         reads=[T_d12f], writes=[T_dei[0]])
    P.op("dve", lambda E: E.tensor_copy(d2i, d12f[:, nrt:2 * nrt]),
         reads=[T_d12f], writes=[T_dei[1]])

    T_xs.writer = [("zf", P.cnt["zf"])]
    T_sidtm.inherit([T_zf])
    for s in range(2):
        P.relay("sp", f"xnt{s}")
    T_xn2tm.writer = [("E_sp", P.cnt["E_sp"])]
    for i in range(nrt):
        s = i % 2
        xn_ap = xnb[:, s * D:(s + 1) * D]
        P.dma("sp", f"xnl{s}", lambda E, i=i, xn_ap=xn_ap: E.dma_start(
            out=xn_ap, in_=d_xn2tm[i * 128:(i + 1) * 128, :]),
            reads=[T_xn2tm], writes=[T_xnb[s]])
        for r, dap in ((0, d1i), (1, d2i)):
            P.dma("gp", f"scx{s}", lambda E, xn_ap=xn_ap, i=i, dap=dap:
                  E.indirect_dma_start(
                      out=d_xs, out_offset=IOA(dap[:, i:i + 1], 0),
                      in_=xn_ap, in_offset=None),
                  reads=[T_xnb[s], T_dei[r], T_xs])
    P.relay("gp", "scx0")
    scx_dep = P.relay("gp", "scx1")
    T_xs.writer = [scx_dep]

    T_xssb = [Tile("xssb0").inherit(T_xn2), Tile("xssb1").inherit(T_xn2)]
    T_xsfm = [Tile("xsfm0").inherit(T_xn2), Tile("xsfm1").inherit(T_xn2)]
    T_yst = [Tile("yst0").inherit(T_xld + [T_d12f]),
             Tile("yst1").inherit(T_xld + [T_d12f])]
    ystot = {0: 0, 1: 0}
    for e in range(NE):
        sl = e % 2
        g0b = 4 * sl
        xss = xssb[:, sl * 2 * D:(sl + 1) * 2 * D]
        xsf = xsfm[:, sl * 2 * D:(sl + 1) * 2 * D].rearrange(
            "p (c t) -> p c t", c=KC)
        ys = yst[:, sl * 2 * D:(sl + 1) * 2 * D]
        P.dma("gp", f"xsl{sl}", lambda E, e=e, xss=xss: E.dma_start(
            out=xss.rearrange("p (t d) -> p t d", t=2),
            in_=d_xs[e * CAP:(e + 1) * CAP, :].rearrange(
                "(t p) d -> p t d", p=128)),
            reads=[T_xs], writes=[T_xssb[sl]])
        for tau in range(2):
            for half in range(2):
                bank = g0b + 2 * tau + half

                def mmt2(E, xss=xss, tau=tau, half=half, bank=bank):
                    last = None
                    for qq in range(4):
                        c = half * 4 + qq
                        last = E.matmul(
                            ps[bank][:, qq * 128:(qq + 1) * 128],
                            xss[:, tau * D + c * 128: tau * D + (c + 1) * 128],
                            ident, start=True, stop=True)
                    return last
                P.op("pe", mmt2, reads=[T_xssb[sl], T_ident],
                     writes=[T_ps[bank]])
                if half == 0:
                    P.op("dve", lambda E, xsf=xsf, tau=tau, half=half, bank=bank:
                         E.tensor_copy(
                             xsf[:, half * 4:half * 4 + 4,
                                 tau * 128:(tau + 1) * 128],
                             ps[bank].rearrange("p (c t) -> p c t", c=4)),
                         reads=[T_ps[bank]], writes=[T_xsfm[sl]])
                else:
                    P.op("act", lambda E, xsf=xsf, tau=tau, half=half, bank=bank:
                         E.activation(
                             out=xsf[:, half * 4:half * 4 + 4,
                                     tau * 128:(tau + 1) * 128],
                             in_=ps[bank].rearrange("p (c t) -> p c t", c=4),
                             func=ACTF.Copy),
                         reads=[T_ps[bank]], writes=[T_xsfm[sl]])
        for half in range(2):
            wsl = (e * 2 + half) % 4
            for tau in range(2):
                bank = g0b + 2 * tau + half

                def mmf2(E, xsf=xsf, tau=tau, half=half, bank=bank, wsl=wsl):
                    last = None
                    for ck in range(KC):
                        last = E.matmul(
                            ps[bank][:, 0:512],
                            xsf[:, ck, tau * 128:(tau + 1) * 128],
                            we[wsl][:, ck * 512:(ck + 1) * 512],
                            start=(ck == 0), stop=(ck == KC - 1),
                            skip_group_check=True)
                    return last
                P.op("pe", mmf2, reads=[T_xsfm[sl], T_we[wsl]],
                     writes=[T_ps[bank]])
                eng = "dve" if half == 0 else "act"
                dst = ys[:, tau * D + half * 512: tau * D + (half + 1) * 512]
                if eng == "dve":
                    P.op("dve", lambda E, dst=dst, bank=bank: E.tensor_copy(
                        dst, ps[bank][:, 0:512]),
                        reads=[T_ps[bank]], writes=[T_yst[sl]])
                else:
                    P.op("act", lambda E, dst=dst, bank=bank: E.activation(
                        out=dst, in_=ps[bank][:, 0:512], func=ACTF.Copy),
                        reads=[T_ps[bank]], writes=[T_yst[sl]])
        for tau in range(2):
            P.dma("sp", f"yst{sl}", lambda E, e=e, tau=tau, ys=ys: E.dma_start(
                out=d_ydr[e * CAP + tau * 128: e * CAP + (tau + 1) * 128, :],
                in_=ys[:, tau * D:(tau + 1) * D]),
                reads=[T_yst[sl]])
        ystot[sl] = P.cnt[f"yst{sl}"]
        if e + 2 < NE:
            for half in range(2):
                wsl = ((e + 2) * 2 + half) % 4
                P.dma("sp", f"we{wsl}", lambda E, e=e, half=half, wsl=wsl:
                      E.dma_start(out=we[wsl], in_=d_ewb[e + 2, :, half, :]),
                      reads=[T_ewb], writes=[T_we[wsl]])
    T_ydr.writer = [("yst0", ystot[0]), ("yst1", ystot[1])]

    T_gb = [Tile("gb0").inherit(T_xn2), Tile("gb1").inherit(T_xn2)]
    for i in range(nrt):
        for r, dap in ((0, d1i), (1, d2i)):
            sl = (2 * i + r) % 2
            gslice = gbuf[:, sl * D:(sl + 1) * D]
            P.dma("gp", f"gb{sl}", lambda E, i=i, dap=dap, gslice=gslice:
                  E.indirect_dma_start(
                      out=gslice, out_offset=None,
                      in_=d_ydr, in_offset=IOA(dap[:, i:i + 1], 0)),
                  reads=[T_ydr, T_dei[r]], writes=[T_gb[sl]])
            P.op("dve", lambda E, i=i, r=r, gslice=gslice:
                 E.scalar_tensor_tensor(
                     out=x2[:, i * D:(i + 1) * D], in0=gslice,
                     scalar=wmx[:, 2 * i + r:2 * i + r + 1],
                     in1=x2[:, i * D:(i + 1) * D],
                     op0=ALU.mult, op1=ALU.add),
                 reads=[T_gb[sl], T_wmx, T_x2[i]], writes=[T_x2[i]])

    for i in range(nrt):
        P.dma("sp", "ost", lambda E, i=i: E.dma_start(
            out=d_out[i * 128:(i + 1) * 128, :], in_=x2[:, i * D:(i + 1) * D]),
            reads=[T_x2[i]])
    P.relay("sp", "ost")

    P.emit()
    return nc


def prep_inputs(L0_attn_x, L0_attn_w_in, L0_attn_b_in, L0_attn_w_conv,
                L0_attn_b_conv, L0_attn_w_xproj, L0_attn_w_dt, L0_attn_b_dt,
                L0_attn_w_out, L0_ffn_router_w, L0_ffn_expert_weights,
                toks=TOKS):
    x = np.asarray(L0_attn_x, dtype=np.float32)
    w_in = np.asarray(L0_attn_w_in, dtype=np.float32)
    b_in = np.asarray(L0_attn_b_in, dtype=np.float32)
    w_conv = np.asarray(L0_attn_w_conv, dtype=np.float32).reshape(DI, 4)
    b_conv = np.asarray(L0_attn_b_conv, dtype=np.float32)
    w_xproj = np.asarray(L0_attn_w_xproj, dtype=np.float32)
    w_dt = np.asarray(L0_attn_w_dt, dtype=np.float32)
    b_dt = np.asarray(L0_attn_b_dt, dtype=np.float32)
    w_out = np.asarray(L0_attn_w_out, dtype=np.float32)
    router_w = np.asarray(L0_ffn_router_w, dtype=np.float32)
    expw = np.asarray(L0_ffn_expert_weights, dtype=np.float32)
    expw = np.ascontiguousarray(
        expw.reshape(NE, KC, 128, 2, 512).transpose(0, 2, 3, 1, 4)
        .reshape(NE, 128, 2, KC * 512))

    w_inT = np.ascontiguousarray(w_in[0:MW, :].T)
    b_in_pad = np.zeros((CC + 1) * 128, np.float32)
    b_in_pad[0:MW] = b_in[0:MW]
    b_in_sb = np.ascontiguousarray(b_in_pad.reshape(CC + 1, 128).T)
    w_conv_sb = np.ascontiguousarray(
        w_conv.reshape(CC, 128, 4).transpose(1, 0, 2).reshape(128, CC * 4))
    b_conv_sb = np.ascontiguousarray(b_conv.reshape(CC, 128).T)
    wxdt = np.concatenate([w_xproj, w_dt], axis=0)
    wxdt_sb = np.ascontiguousarray(
        wxdt.T.reshape(CC, 128, 48).transpose(1, 0, 2).reshape(128, CC * 48))
    w_out16T = np.ascontiguousarray(w_out[:, 0:NST].T)
    router_sb = np.ascontiguousarray(
        router_w.T.reshape(KC, 128, NE).transpose(1, 0, 2).reshape(128, KC * NE))
    b_dt_sb = np.ascontiguousarray(b_dt.reshape(NST, 1))

    shared = dict(w_inT=w_inT, b_in_sb=b_in_sb, w_conv_sb=w_conv_sb,
                  b_conv_sb=b_conv_sb, wxdt_sb=wxdt_sb, b_dt=b_dt_sb,
                  w_out16T=w_out16T, router_sb=router_sb)

    in_maps = []
    for k in range(NCORES):
        b, c = k // CPS, k % CPS
        if c == 0:
            xs = np.concatenate([np.zeros((128, D), np.float32),
                                 x[b, 0:toks]], axis=0)
            halo = np.zeros((128, 3), np.float32)
        else:
            xs = x[b, c * toks - 128: (c + 1) * toks]
            halo = np.ones((128, 3), np.float32)
        m = np.zeros((NST, NCORES), np.float32)
        for j in range((k // CPS) * CPS, k):
            m[:, j] = 1.0
        im = dict(shared)
        im["x"] = np.ascontiguousarray(xs)
        im["halo_mask"] = halo
        im["m_all"] = m
        rot = (k * 4) % NE
        im["expw"] = np.ascontiguousarray(np.roll(expw, -rot, axis=0))
        im["ecap1"] = (((np.arange(NE) - rot) % NE) * CAP - 1.0).astype(
            np.float32).reshape(NE, 1)
        in_maps.append(im)
    return in_maps


def kernel(**inputs):
    in_maps = prep_inputs(**inputs)
    nc = build_nc()
    res = run_bass_kernel_spmd(nc, in_maps, core_ids=list(range(NCORES)))
    out = np.empty((BATCH, SEQ, D), np.float32)
    for k in range(NCORES):
        b, c = k // CPS, k % CPS
        out[b, c * TOKS:(c + 1) * TOKS, :] = res.results[k]["out"]
    return out


# version 23
# speedup vs baseline: 1.4280x; 1.4280x over previous
import contextlib
import numpy as np

import concourse.bass as bass
import concourse.mybir as mybir
from concourse.bass_utils import run_bass_kernel_spmd

F32 = mybir.dt.float32
BF16 = mybir.dt.bfloat16
ALU = mybir.AluOpType
ACTF = mybir.ActivationFunctionType
AX = mybir.AxisListType

D = 1024
DI = 2048
NST = 16
BATCH, SEQ = 2, 8192
NCORES = 8
CPS = 4
TOKS = SEQ // CPS
TT = TOKS + 128
NT = TT // 128
NRT = TOKS // 128
KC = D // 128
CC = DI // 128
NE = 32
MW = DI + NST
EPS = 1e-5
CAP = 256
NSLOT = NE * CAP


class Tile:
    __slots__ = ("name", "writer", "readers")

    def __init__(self, name):
        self.name = name
        self.writer = []
        self.readers = []

    def inherit(self, parents):
        deps = []
        for p in parents:
            deps.extend(p.writer)
            deps.extend(p.readers)
        self.readers = deps
        return self


class Prog:
    ENGS = ("pe", "dve", "act", "gp", "sp")

    def __init__(self, nc):
        self.nc = nc
        self.ths = {e: [] for e in self.ENGS}
        self.ctx = contextlib.ExitStack()
        self.semh = {}
        self.cnt = {}
        self.satisfied = {e: {} for e in self.ENGS}
        self.esem = {e: "E_" + e for e in self.ENGS}

    def sem(self, name):
        if name not in self.semh:
            self.semh[name] = self.ctx.enter_context(self.nc.semaphore(name))
            self.cnt[name] = 0
        return self.semh[name]

    def _emit_wait(self, eng, sname, val):
        if val <= 0:
            return
        if self.satisfied[eng].get(sname, 0) >= val:
            return
        self.satisfied[eng][sname] = val
        h = self.sem(sname)
        self.ths[eng].append(lambda E, h=h, v=val: E.wait_ge(h, v))

    def _deps(self, eng, reads, writes):
        need = {}
        for t in reads:
            for s, v in t.writer:
                need[s] = max(need.get(s, 0), v)
        for t in writes:
            for s, v in t.writer:
                need[s] = max(need.get(s, 0), v)
            for s, v in t.readers:
                need[s] = max(need.get(s, 0), v)
        for s, v in sorted(need.items()):
            self._emit_wait(eng, s, v)

    def op(self, eng, fn, reads=(), writes=(), sem=None):
        self._deps(eng, reads, writes)
        sname = sem or self.esem[eng]
        h = self.sem(sname)
        self.ths[eng].append(lambda E, h=h, fn=fn: fn(E).then_inc(h, 1))
        self.cnt[sname] += 1
        v = self.cnt[sname]
        for t in reads:
            t.readers.append((sname, v))
        for t in writes:
            t.writer = [(sname, v)]
            t.readers = []

    def dma(self, eng, sname, fn, reads=(), writes=()):
        self._deps(eng, reads, writes)
        h = self.sem(sname)
        self.ths[eng].append(lambda E, h=h, fn=fn: fn(E).then_inc(h, 16))
        self.cnt[sname] += 16
        v = self.cnt[sname]
        for t in reads:
            t.readers.append((sname, v))
        for t in writes:
            t.writer = [(sname, v)]
            t.readers = []

    def relay(self, eng, sname):
        v = self.cnt[sname]
        self._emit_wait(eng, sname, v)
        es = self.esem[eng]
        h = self.sem(es)
        self.ths[eng].append(lambda E, h=h: E.nop().then_inc(h, 1))
        self.cnt[es] += 1
        return (es, self.cnt[es])

    def emit(self):
        nc = self.nc
        block = self.ctx.enter_context(nc.Block())
        engobj = {"pe": block.tensor, "dve": block.vector, "act": block.scalar,
                  "gp": block.gpsimd, "sp": block.sync}
        for e in self.ENGS:
            ths = self.ths[e]

            def body(E, ths=ths):
                for t in ths:
                    t(E)
            engobj[e](body)


def build_nc(toks=TOKS):
    nt = toks // 128 + 1
    nrt = toks // 128
    tt = toks + 128

    nc = bass.Bass("TRN2", target_bir_lowering=False, debug=False)
    P = Prog(nc)

    d_x = nc.dram_tensor("x", [tt, D], F32, kind="ExternalInput").ap()
    d_win = nc.dram_tensor("w_inT", [D, MW], F32, kind="ExternalInput").ap()
    d_bin = nc.dram_tensor("b_in_sb", [128, CC + 1], F32, kind="ExternalInput").ap()
    d_wc = nc.dram_tensor("w_conv_sb", [128, CC * 4], F32, kind="ExternalInput").ap()
    d_bc = nc.dram_tensor("b_conv_sb", [128, CC], F32, kind="ExternalInput").ap()
    d_wxdt = nc.dram_tensor("wxdt_sb", [128, CC * 48], F32, kind="ExternalInput").ap()
    d_bdt = nc.dram_tensor("b_dt", [NST, 1], F32, kind="ExternalInput").ap()
    d_wo = nc.dram_tensor("w_out16T", [NST, D], F32, kind="ExternalInput").ap()
    d_rw = nc.dram_tensor("router_sb", [128, KC * NE], F32, kind="ExternalInput").ap()
    d_ew = nc.dram_tensor("expw", [NE, 128, 2, KC * 512], F32, kind="ExternalInput").ap()
    d_halo = nc.dram_tensor("halo_mask", [128, 3], F32, kind="ExternalInput").ap()
    d_mall = nc.dram_tensor("m_all", [NST, NCORES], F32, kind="ExternalInput").ap()
    d_out = nc.dram_tensor("out", [toks, D], F32, kind="ExternalOutput").ap()

    d_ecap1 = nc.dram_tensor("ecap1", [NE, 1], F32, kind="ExternalInput").ap()
    d_xn2tm = nc.dram_tensor("xn2tm", [toks, D], BF16).ap()
    d_xs = nc.dram_tensor("xs", [NSLOT, D], BF16).ap()
    d_ydr = nc.dram_tensor("ydr", [NSLOT, D], BF16).ap()
    T_xn2tm = Tile("xn2tm")
    T_xs = Tile("xs")
    T_ydr = Tile("ydr")
    d_cci = nc.dram_tensor("cc_in", [2 * NST], F32).ap()
    d_cco = nc.dram_tensor("cc_out", [NCORES, 2 * NST], F32).ap()
    t_cci = Tile("cc_in")
    t_cco = Tile("cc_out")

    A = nc.alloc_sbuf_tensor
    AT = nc.alloc_sbuf_tensor_at

    def addr(h):
        return nc.lookup_mloc(h).addr

    x2_h = A("x2", [128, nrt * D], F32)
    xnfm_h = A("xnfm", [128, KC * tt], BF16)
    win_h = A("win", [128, KC * MW], BF16)
    xzb_h = A("xzb", [128, tt], F32)
    cvb_h = A("cvb", [128, tt], F32)
    xcr_h = A("xcr", [128, 2 * toks], BF16)
    xnb_h = A("xnb", [128, 2 * D], BF16)
    xld_h = A("xld", [128, 4 * D], F32)
    xc16_h = A("xc16", [NST, toks], F32)
    z16_h = A("z16", [NST, tt], F32)
    xdt_h = A("xdt48", [48, toks], F32)
    wm_h = A("wm", [128, nrt * NE], F32)
    identf_h = A("identf", [32, 32], F32)
    ecap_h2 = A("ecapc", [NE, 1], F32)
    wmx_h = A("wmx", [128, 2 * nrt], F32)
    d12f_h = A("d12f", [128, 2 * nrt], F32)
    d1i_h = A("d1i", [128, nrt], mybir.dt.int32)
    d2i_h = A("d2i", [128, nrt], mybir.dt.int32)
    m12b_h = A("m12b", [128, NE], BF16)
    st_h = A("stats", [128, 64], F32)
    bin_h = A("binc", [128, CC + 1], F32)
    wc_h = A("wcc", [128, CC * 4], F32)
    bc_h = A("bcc", [128, CC], F32)
    wxdt_h = A("wxdtc", [128, CC * 48], BF16)
    bdt_h = A("bdtc", [NST, 1], F32)
    wo_h = A("woc", [NST, D], BF16)
    rw_h = A("rwc", [128, KC * NE], BF16)
    halo_h = A("haloc", [128, 3], F32)
    mall_h = A("mallc", [NST, NCORES], F32)
    ident_h = A("identc", [128, 128], BF16)
    sm_h = A("scansm", [NST, 32], F32)
    pa_h = A("pa_all", [NST, NCORES], F32)
    he_h = A("he_all", [NST, NCORES], F32)
    pam_h = A("pam", [NST, NCORES], F32)
    hem_h = A("hem", [NST, NCORES], F32)
    st8_h = A("st8", [NST, NCORES], F32)

    if toks * 4 >= D * 4:
        sqd_h = AT("sqd", [128, D], F32, offset=addr(xdt_h))
    else:
        sqd_h = A("sqd", [128, D], F32)
    rsc_h = AT("rsc", [128, 96], F32, offset=addr(xdt_h) + 4096)
    t_h_h = AT("t_h", [NST, toks], F32, offset=addr(xnfm_h))
    y16b_h = AT("y16b", [NST, toks], BF16, offset=addr(xnfm_h) + toks * 4)
    xn2fm_h = AT("xn2fm", [128, KC * toks], BF16, offset=addr(xnfm_h))
    we_hs = [AT(f"we{q}", [128, KC * 512], BF16,
                offset=addr(win_h) + q * KC * 512 * 2) for q in range(4)]
    assert 4 * KC * 512 * 2 <= KC * MW * 2
    txb_h = AT("t_xdblB", [NST, toks], F32, offset=addr(xzb_h))
    tdl_h = AT("t_delta", [NST, toks], F32, offset=addr(cvb_h))
    xzb2_h = AT("xzb2", [128, tt], F32, offset=addr(x2_h))
    cvb2_h = AT("cvb2", [128, tt], F32, offset=addr(x2_h) + tt * 4)
    ttm_h = AT("t_tmp", [NST, toks], F32, offset=addr(xcr_h))

    m12t_h = AT("m12t", [NE, toks], F32, offset=addr(xzb_h))
    sid_h = AT("sid", [NE, toks], F32, offset=addr(z16_h))
    m1a_h = AT("m1all", [128, nrt * NE], F32, offset=addr(cvb_h))
    m2a_h = AT("m2all", [128, nrt * NE], F32,
               offset=addr(cvb_h) + nrt * NE * 4)
    assert 2 * nrt * NE * 4 <= tt * 4
    yst_h = AT("ystage", [128, 2 * 2 * D], BF16, offset=addr(xld_h))
    wreg = addr(xnfm_h)
    wsz = KC * tt * 2
    woff = [0]

    def at_w(name, shape, dtype):
        nbytes = 1
        for s_ in shape[1:]:
            nbytes *= s_
        nbytes *= (2 if dtype == BF16 else 4)
        if woff[0] + nbytes <= wsz:
            h = AT(name, shape, dtype, offset=wreg + woff[0])
            woff[0] += nbytes
        else:
            h = A(name, shape, dtype)
        return h
    xssb_h = at_w("xssb", [128, 2 * 2 * D], BF16)
    xsfm_h = at_w("xsfm", [128, 2 * 2 * D], BF16)
    gbuf_h = at_w("gbuf", [128, 4 * D], BF16)
    stg_hs = [AT("stgA", [128, 2048], F32, offset=addr(xc16_h)),
              AT("stgB", [128, 2048], F32, offset=addr(z16_h)),
              AT("stgC", [128, 2048], F32, offset=addr(xcr_h)),
              at_w("stgD", [128, 2048], F32)]
    stg2_hs = [AT("stgE", [128, 2048], F32, offset=addr(xzb_h)),
               AT("stgF", [128, 2048], F32, offset=addr(cvb_h)),
               AT("stgG", [128, 2048], F32, offset=addr(xdt_h)),
               AT("stgH", [128, 2048], F32, offset=addr(xld_h) + 8192)]
    assert woff[0] <= wsz

    rem = nc.sbuf_bytes_remaining
    assert rem >= 0, f"SBUF over budget: {rem}"

    ps = [nc.alloc_psum_tensor(f"ps{i}", [128, 512], F32).ap() for i in range(8)]

    x2 = x2_h.ap()
    xnfm = xnfm_h.ap()
    win = win_h.ap()
    xzb = xzb_h.ap()
    cvb = cvb_h.ap()
    xcr = xcr_h.ap()
    xnb = xnb_h.ap()
    xld = xld_h.ap()
    xc16 = xc16_h.ap()
    z16 = z16_h.ap()
    xdt = xdt_h.ap()
    wm = wm_h.ap()
    sqd = sqd_h.ap()
    stats = st_h.ap()
    rsc = rsc_h.ap()
    binc = bin_h.ap()
    wcc = wc_h.ap()
    bcc = bc_h.ap()
    wxdtc = wxdt_h.ap()
    bdtc = bdt_h.ap()
    woc = wo_h.ap()
    rwc = rw_h.ap()
    haloc = halo_h.ap()
    mallc = mall_h.ap()
    ident = ident_h.ap()
    sm = sm_h.ap()
    pa_all = pa_h.ap()
    he_all = he_h.ap()
    pam = pam_h.ap()
    hem = hem_h.ap()
    st8 = st8_h.ap()
    identf = identf_h.ap()
    ecapc = ecap_h2.ap()
    wmx = wmx_h.ap()
    d12f = d12f_h.ap()
    d1i = d1i_h.ap()
    d2i = d2i_h.ap()
    m12b = m12b_h.ap()
    m12t = m12t_h.ap()
    sid = sid_h.ap()
    m1a = m1a_h.ap()
    m2a = m2a_h.ap()
    yst = yst_h.ap()
    xssb = xssb_h.ap()
    xsfm = xsfm_h.ap()
    gbuf = gbuf_h.ap()
    stg = [h.ap() for h in stg_hs]
    stg2 = [h.ap() for h in stg2_hs]
    t_h = t_h_h.ap()
    y16b = y16b_h.ap()
    xn2fm = xn2fm_h.ap()
    we = [h.ap() for h in we_hs]
    txb = txb_h.ap()
    tdl = tdl_h.ap()
    ttm = ttm_h.ap()
    xzbs = [xzb, xzb2_h.ap()]
    cvbs = [cvb, cvb2_h.ap()]

    T_x2 = [Tile(f"x2_{i}") for i in range(nrt)]
    T_xnfm = Tile("xnfm")
    T_xnf = [Tile(f"xnf{j}") for j in range(nt)]
    T_win = Tile("win")
    T_xzbs = [Tile("xzb0"), Tile("xzb1")]
    T_cvbs = [Tile("cvb0"), Tile("cvb1")]
    T_xcr = [Tile("xcr0"), Tile("xcr1")]
    T_xnb = [Tile("xnb0"), Tile("xnb1")]
    T_xld = [Tile(f"xld{i}") for i in range(4)]
    T_xc16 = Tile("xc16")
    T_z16 = Tile("z16")
    T_xdt = Tile("xdt")
    T_sidtm = Tile("sidtm")
    T_sqd = Tile("sqd")
    T_stats = [Tile("stats0"), Tile("stats1")]
    T_consts = Tile("consts")
    T_ident = Tile("ident")
    T_ps = [Tile(f"ps{i}") for i in range(8)]
    T_smalls = Tile("scansmalls")
    T_pa = Tile("pa_all")
    T_he = Tile("he_all")
    T_pam = Tile("pam")
    T_hem = Tile("hem")
    T_st8 = Tile("st8")

    P.op("gp", lambda E: E.memset(ident, 0.0), writes=[T_ident])
    P.op("gp", lambda E: E.affine_select(
        out=ident, in_=ident, compare_op=ALU.not_equal, fill=1.0, base=0,
        pattern=[[-1, 128]], channel_multiplier=1),
        reads=[T_ident], writes=[T_ident])

    T_identf = Tile("identf")
    P.op("gp", lambda E: E.memset(identf, 0.0), writes=[T_identf])
    P.op("gp", lambda E: E.affine_select(
        out=identf, in_=identf, compare_op=ALU.not_equal, fill=1.0, base=0,
        pattern=[[-1, 32]], channel_multiplier=1),
        reads=[T_identf], writes=[T_identf])

    d_wci = nc.dram_tensor("wm_cci", [NST * 2], F32).ap()
    d_wco = nc.dram_tensor("wm_cco", [NCORES, NST * 2], F32).ap()
    t_wci = Tile("wm_cci")
    P.dma("gp", "wrm", lambda E: E.dma_start(out=d_wci, in_=d_mall[:, 0:2]),
          writes=[t_wci])
    P.op("gp", lambda E: E.collective_compute(
        "AllGather", ALU.bypass, replica_groups=[list(range(NCORES))],
        ins=[d_wci.opt()], outs=[d_wco.opt()]),
        reads=[t_wci], writes=[Tile("wm_cco")], sem="ccw")

    P.dma("gp", "G0", lambda E: E.dma_start(
        out=win.rearrange("p (c m) -> p c m", c=KC),
        in_=d_win.rearrange("(c p) m -> p c m", p=128)))
    P.dma("gp", "G0", lambda E: E.dma_start(out=wxdtc, in_=d_wxdt))
    P.dma("gp", "G0", lambda E: E.dma_start(out=woc, in_=d_wo))
    P.dma("gp", "G0", lambda E: E.dma_start(out=rwc, in_=d_rw))
    gdep = P.relay("gp", "G0")
    T_win.writer = [gdep]
    T_bf16c = Tile("bf16consts")
    T_bf16c.writer = [gdep]

    inv_d = 1.0 / D

    def ln_stats(x_ap, T_xsrc, g, gi, T_stat):
        col = g * 32
        P.op("dve", lambda E: E.reduce_sum(stats[:, col + gi:col + gi + 1],
                                           x_ap, axis=AX.X),
             reads=[T_xsrc], writes=[T_stat])
        P.op("act", lambda E: E.activation(out=sqd, in_=x_ap, func=ACTF.Square,
                                           accum_out=stats[:, col + 4 + gi:col + 5 + gi]),
             reads=[T_xsrc], writes=[T_sqd, T_stat])

    def ln_chain(g, n, T_stat):
        col = g * 32
        ssum = stats[:, col:col + n]
        ss2 = stats[:, col + 4:col + 4 + n]
        mean = stats[:, col + 8:col + 8 + n]
        ex2 = stats[:, col + 12:col + 12 + n]
        var = stats[:, col + 16:col + 16 + n]
        rstd = stats[:, col + 20:col + 20 + n]
        nbias = stats[:, col + 24:col + 24 + n]
        P.op("dve", lambda E: E.tensor_scalar(out=mean, in0=ssum, scalar1=inv_d,
                                              scalar2=None, op0=ALU.mult),
             reads=[T_stat], writes=[T_stat])
        P.op("dve", lambda E: E.tensor_scalar(out=ex2, in0=ss2, scalar1=inv_d,
                                              scalar2=None, op0=ALU.mult),
             reads=[T_stat], writes=[T_stat])
        P.op("dve", lambda E: E.tensor_tensor(out=var, in0=mean, in1=mean,
                                              op=ALU.mult),
             reads=[T_stat], writes=[T_stat])
        P.op("dve", lambda E: E.tensor_sub(var, var, ex2),
             reads=[T_stat], writes=[T_stat])
        P.op("dve", lambda E: E.tensor_scalar(out=var, in0=var, scalar1=EPS,
                                              scalar2=-1.0, op0=ALU.subtract,
                                              op1=ALU.mult),
             reads=[T_stat], writes=[T_stat])
        P.op("act", lambda E: E.activation(out=var, in_=var, func=ACTF.Sqrt),
             reads=[T_stat], writes=[T_stat])
        P.op("dve", lambda E: E.reciprocal(rstd, var),
             reads=[T_stat], writes=[T_stat])
        P.op("dve", lambda E: E.tensor_scalar(out=nbias, in0=mean, scalar1=-1.0,
                                              scalar2=None, op0=ALU.mult),
             reads=[T_stat], writes=[T_stat])
        P.op("dve", lambda E: E.tensor_mul(nbias, nbias, rstd),
             reads=[T_stat], writes=[T_stat])

    def ln_norm(x_ap, T_xsrc, g, gi, T_stat, xn_ap, T_xn):
        col = g * 32
        rstd = stats[:, col + 20 + gi:col + 21 + gi]
        nbias = stats[:, col + 24 + gi:col + 25 + gi]
        P.op("act", lambda E: E.activation(out=xn_ap, in_=x_ap,
                                           func=ACTF.Identity,
                                           scale=rstd, bias=nbias),
             reads=[T_xsrc, T_stat], writes=[T_xn])

    def transpose_tile(xn_ap, T_xn, dst_base_col, dst_stride, T_dst, dst_ap_fn,
                       dr=("dve", "act")):
        for half in range(2):
            pst = T_ps[half]
            psa = ps[half]

            def mmt(E, half=half, xn_ap=xn_ap, psa=psa):
                last = None
                for q in range(4):
                    c = half * 4 + q
                    last = E.matmul(psa[:, q * 128:(q + 1) * 128],
                                    xn_ap[:, c * 128:(c + 1) * 128],
                                    ident, start=True, stop=True)
                return last
            P.op("pe", mmt, reads=[T_xn, T_ident], writes=[pst])
            if dr[half] == "dve":
                P.op("dve", lambda E, half=half, psa=psa: E.tensor_copy(
                    dst_ap_fn(half), psa.rearrange("p (c t) -> p c t", c=4)),
                    reads=[pst], writes=[T_dst])
            else:
                P.op("act", lambda E, half=half, psa=psa: E.activation(
                    out=dst_ap_fn(half),
                    in_=psa.rearrange("p (c t) -> p c t", c=4),
                    func=ACTF.Copy),
                    reads=[pst], writes=[T_dst])

    for g0 in range(0, nt, 4):
        gn = min(4, nt - g0)
        g = (g0 // 4) % 2
        for gi in range(gn):
            tau = g0 + gi
            r = tau % 4
            x_ap = xld[:, r * D:(r + 1) * D]
            P.dma("sp", f"xr{r}", lambda E, x_ap=x_ap, tau=tau: E.dma_start(
                out=x_ap, in_=d_x[tau * 128:(tau + 1) * 128, :]),
                writes=[T_xld[r]])
            ln_stats(x_ap, T_xld[r], g, gi, T_stats[g])
        ln_chain(g, gn, T_stats[g])
        for gi in range(gn):
            tau = g0 + gi
            r = tau % 4
            s = tau % 2
            x_ap = xld[:, r * D:(r + 1) * D]
            xn_ap = xnb[:, s * D:(s + 1) * D]
            ln_norm(x_ap, T_xld[r], g, gi, T_stats[g], xn_ap, T_xnb[s])

            def dstfn(half, tau=tau):
                return xnfm.rearrange("p (c t) -> p c t", c=KC)[
                    :, half * 4:half * 4 + 4, tau * 128:(tau + 1) * 128]
            transpose_tile(xn_ap, T_xnb[s], None, None, T_xnf[tau], dstfn,
                           dr=("dve", "dve"))

    for dst, srcp in ((binc, d_bin), (wcc, d_wc), (bcc, d_bc), (bdtc, d_bdt),
                      (haloc, d_halo), (mallc, d_mall), (ecapc, d_ecap1)):
        P.dma("sp", "L0", lambda E, dst=dst, srcp=srcp: E.dma_start(out=dst, in_=srcp))
    P.relay("sp", "L0")
    T_consts.writer = [("E_sp", P.cnt["E_sp"])]

    tqs = []
    c0 = 0
    while c0 < tt:
        w = min(512, tt - c0)
        tqs.append((c0, w))
        c0 += w

    pend = None

    def emit_sil_xp(pend):
        mc, xc_ap, cv_ap, T_cv, s = pend
        P.op("act", lambda E, mc=mc, xc_ap=xc_ap, cv_ap=cv_ap: E.activation(
            out=xc_ap, in_=cv_ap[:, 128:tt], func=ACTF.Silu,
            bias=bcc[:, mc:mc + 1], scale=1.0),
            reads=[T_cv, T_consts], writes=[T_xcr[s]])
        if mc == 0:
            P.op("act", lambda E, cv_ap=cv_ap: E.activation(
                out=xc16, in_=cv_ap[0:NST, 128:tt], func=ACTF.Silu,
                bias=bcc[0:NST, 0:1], scale=1.0),
                reads=[T_cv, T_consts], writes=[T_xc16])

        def mmx(E, mc=mc, xc_ap=xc_ap):
            last = None
            for q in range(toks // 512):
                last = E.matmul(
                    ps[4 + q][0:48, :],
                    wxdtc.rearrange("p (c m) -> p c m", c=CC)[:, mc, :],
                    xc_ap[:, q * 512:(q + 1) * 512],
                    start=(mc == 0), stop=(mc == CC - 1),
                    skip_group_check=True)
            return last
        P.op("pe", mmx, reads=[T_bf16c, T_xcr[s]],
             writes=[T_ps[4 + q] for q in range(toks // 512)])

    for mc in range(CC + 1):
        is_z = mc == CC
        mwid = NST if is_z else 128
        moff = mc * 128
        s = mc % 2
        xz_ap = xzbs[s]
        cv_ap = cvbs[s]
        T_xz = T_xzbs[s]
        T_cv = T_cvbs[s]
        for qi, (t0, w) in enumerate(tqs):
            bank = 2 + (qi % 2)
            pst = T_ps[bank]
            psa = ps[bank]

            def mmw(E, mc=mc, mwid=mwid, moff=moff, t0=t0, w=w, psa=psa):
                last = None
                for ck in range(KC):
                    last = E.matmul(
                        psa[0:mwid, 0:w],
                        win.rearrange("p (c m) -> p c m", c=KC)[:, ck, moff:moff + mwid],
                        xnfm[:, ck * tt + t0: ck * tt + t0 + w],
                        start=(ck == 0), stop=(ck == KC - 1))
                return last
            P.op("pe", mmw, reads=[T_win] + T_xnf, writes=[pst])
            if is_z:
                P.op("act", lambda E, t0=t0, w=w, psa=psa: E.activation(
                    out=z16[:, t0:t0 + w], in_=psa[0:NST, 0:w],
                    func=ACTF.Identity,
                    bias=binc[0:NST, CC:CC + 1], scale=1.0),
                    reads=[pst, T_consts], writes=[T_z16])
            else:
                P.op("act", lambda E, t0=t0, w=w, psa=psa, mc=mc, xz_ap=xz_ap:
                     E.activation(
                         out=xz_ap[:, t0:t0 + w], in_=psa[:, 0:w],
                         func=ACTF.Identity,
                         bias=binc[:, mc:mc + 1], scale=1.0),
                     reads=[pst, T_consts], writes=[T_xz])
        if is_z:
            continue
        P.op("dve", lambda E, xz_ap=xz_ap: E.tensor_mul(
            xz_ap[:, 125:128], xz_ap[:, 125:128], haloc),
            reads=[T_xz, T_consts], writes=[T_xz])
        P.op("dve", lambda E, mc=mc, xz_ap=xz_ap, cv_ap=cv_ap: E.tensor_scalar(
            out=cv_ap[:, 125:tt], in0=xz_ap[:, 122:tt - 3],
            scalar1=wcc[:, mc * 4:mc * 4 + 1], scalar2=None, op0=ALU.mult),
            reads=[T_xz, T_consts], writes=[T_cv])
        for k in (1, 2, 3):
            P.op("dve", lambda E, mc=mc, k=k, xz_ap=xz_ap, cv_ap=cv_ap:
                 E.scalar_tensor_tensor(
                     out=cv_ap[:, 125:tt], in0=xz_ap[:, 122 + k:tt - 3 + k],
                     scalar=wcc[:, mc * 4 + k:mc * 4 + k + 1],
                     in1=cv_ap[:, 125:tt],
                     op0=ALU.mult, op1=ALU.add),
                 reads=[T_xz, T_cv, T_consts], writes=[T_cv])
        if pend is not None:
            emit_sil_xp(pend)
        pend = (mc, xcr[:, s * toks:(s + 1) * toks], cv_ap, T_cv, s)
    emit_sil_xp(pend)

    nq = toks // 512
    for q in range(nq):
        P.op("dve", lambda E, q=q: E.tensor_copy(xdt[:, q * 512:(q + 1) * 512],
                                                 ps[4 + q][0:48, :]),
             reads=[T_ps[4 + q]], writes=[T_xdt])
    T_txb = Tile("t_xdblB").inherit([T_xzbs[0]])
    T_tdl = Tile("t_delta").inherit([T_cvbs[0]])
    T_ttm = Tile("t_tmp").inherit(T_xcr)
    P.dma("sp", "sh1", lambda E: E.dma_start(out=txb, in_=xdt[16:32, :]),
          reads=[T_xdt], writes=[T_txb])
    P.dma("sp", "sh2", lambda E: E.dma_start(out=tdl, in_=xdt[32:48, :]),
          reads=[T_xdt], writes=[T_tdl])
    Aap = xdt[0:NST, :]
    P.op("act", lambda E: E.activation(out=Aap, in_=Aap, func=ACTF.Abs),
         reads=[T_xdt], writes=[T_xdt])
    P.op("act", lambda E: E.activation(out=ttm, in_=tdl, func=ACTF.Abs,
                                       bias=bdtc[:, 0:1], scale=1.0),
         reads=[T_tdl, T_consts], writes=[T_ttm])
    P.op("act", lambda E: E.activation(out=ttm, in_=ttm, func=ACTF.Exp,
                                       scale=-1.0),
         reads=[T_ttm], writes=[T_ttm])
    P.op("act", lambda E: E.activation(out=ttm, in_=ttm, func=ACTF.Ln,
                                       bias=1.0, scale=1.0),
         reads=[T_ttm], writes=[T_ttm])
    P.op("act", lambda E: E.activation(out=tdl, in_=tdl, func=ACTF.Relu,
                                       bias=bdtc[:, 0:1], scale=1.0),
         reads=[T_tdl, T_consts], writes=[T_tdl])
    P.op("dve", lambda E: E.tensor_add(tdl, tdl, ttm),
         reads=[T_tdl, T_ttm], writes=[T_tdl])
    P.op("dve", lambda E: E.tensor_mul(Aap, Aap, tdl),
         reads=[T_xdt, T_tdl], writes=[T_xdt])
    P.op("dve", lambda E: E.reduce_sum(sm[:, 0:1], Aap, axis=AX.X),
         reads=[T_xdt], writes=[T_smalls])
    P.op("act", lambda E: E.activation(out=sm[:, 0:1], in_=sm[:, 0:1],
                                       func=ACTF.Exp, scale=-1.0),
         reads=[T_smalls], writes=[T_smalls])
    P.op("act", lambda E: E.activation(out=Aap, in_=Aap, func=ACTF.Exp,
                                       scale=-1.0),
         reads=[T_xdt], writes=[T_xdt])
    P.op("dve", lambda E: E.tensor_mul(txb, txb, tdl),
         reads=[T_txb, T_tdl], writes=[T_txb])
    P.op("dve", lambda E: E.tensor_mul(txb, txb, xc16),
         reads=[T_txb, T_xc16], writes=[T_txb])
    T_th = Tile("t_h").inherit(T_xnf)
    P.op("dve", lambda E: E.tensor_tensor_scan(out=t_h, data0=Aap, data1=txb,
                                               initial=0.0, op0=ALU.mult,
                                               op1=ALU.add),
         reads=[T_xdt, T_txb], writes=[T_th])
    P.dma("sp", "cci", lambda E: E.dma_start(out=d_cci[0:NST], in_=sm[:, 0:1]),
          reads=[T_smalls], writes=[t_cci])
    P.dma("sp", "cci", lambda E: E.dma_start(out=d_cci[NST:2 * NST],
                                             in_=t_h[:, toks - 1:toks]),
          reads=[T_th], writes=[t_cci])
    P.op("gp", lambda E: E.collective_compute(
        "AllGather", ALU.bypass, replica_groups=[list(range(NCORES))],
        ins=[d_cci.opt()], outs=[d_cco.opt()]),
        reads=[t_cci], writes=[t_cco], sem="cc")
    T_we = [Tile(f"we{q}").inherit([T_win]) for q in range(4)]
    for q in range(4):
        P.dma("gp", f"we{q}", lambda E, q=q: E.dma_start(
            out=we[q], in_=d_ew[q // 2, :, q % 2, :]),
            writes=[T_we[q]])
    P.op("dve", lambda E: E.memset(xc16, 0.0),
         reads=[T_xc16], writes=[T_xc16])
    P.op("dve", lambda E: E.tensor_tensor_scan(out=ttm, data0=Aap, data1=xc16,
                                               initial=1.0, op0=ALU.mult,
                                               op1=ALU.add),
         reads=[T_xdt, T_xc16, T_ttm], writes=[T_ttm])
    P.op("dve", lambda E: E.tensor_mul(t_h, t_h, z16[:, 128:tt]),
         reads=[T_th, T_z16], writes=[T_th])
    P.op("dve", lambda E: E.tensor_mul(ttm, ttm, z16[:, 128:tt]),
         reads=[T_ttm, T_z16], writes=[T_ttm])
    P.dma("sp", "ccl", lambda E: E.dma_start(
        out=pa_all, in_=d_cco[:, 0:NST].rearrange("c s -> s c"),
        allow_slow_non_contiguous=True),
        reads=[t_cco], writes=[T_pa])
    P.dma("sp", "ccl2", lambda E: E.dma_start(
        out=he_all, in_=d_cco[:, NST:2 * NST].rearrange("c s -> s c"),
        allow_slow_non_contiguous=True),
        reads=[t_cco], writes=[T_he])
    P.op("dve", lambda E: E.tensor_scalar(out=pam, in0=pa_all, scalar1=1.0,
                                          scalar2=None, op0=ALU.subtract),
         reads=[T_pa], writes=[T_pam])
    P.op("dve", lambda E: E.tensor_mul(pam, pam, mallc),
         reads=[T_pam, T_consts], writes=[T_pam])
    P.op("dve", lambda E: E.tensor_scalar(out=pam, in0=pam, scalar1=1.0,
                                          scalar2=None, op0=ALU.add),
         reads=[T_pam], writes=[T_pam])
    P.op("dve", lambda E: E.tensor_mul(hem, he_all, mallc),
         reads=[T_he, T_consts], writes=[T_hem])
    P.op("dve", lambda E: E.tensor_tensor_scan(out=st8, data0=pam, data1=hem,
                                               initial=0.0, op0=ALU.mult,
                                               op1=ALU.add),
         reads=[T_pam, T_hem], writes=[T_st8])
    T_y16b = Tile("y16b").inherit(T_xnf)
    P.op("dve", lambda E: E.scalar_tensor_tensor(
        out=t_h, in0=ttm, scalar=st8[:, NCORES - 1:NCORES], in1=t_h,
        op0=ALU.mult, op1=ALU.add),
        reads=[T_ttm, T_st8, T_th], writes=[T_th])
    P.op("dve", lambda E: E.tensor_copy(y16b, t_h),
         reads=[T_th], writes=[T_y16b])

    for i in range(nrt):
        if i * D * 4 < 2 * tt * 4:
            T_x2[i].inherit([T_xzbs[1], T_cvbs[1]])
        tau = i + 1
        r = tau % 4
        x_ap = xld[:, r * D:(r + 1) * D]
        P.dma("sp", f"xr{r}", lambda E, x_ap=x_ap, tau=tau: E.dma_start(
            out=x_ap, in_=d_x[tau * 128:(tau + 1) * 128, :]),
            writes=[T_xld[r]])

        def mmo(E, i=i):
            E.matmul(ps[2], y16b[:, i * 128:(i + 1) * 128], woc[:, 0:512],
                     start=True, stop=True)
            return E.matmul(ps[3], y16b[:, i * 128:(i + 1) * 128],
                            woc[:, 512:1024], start=True, stop=True)
        P.op("pe", mmo, reads=[T_y16b, T_bf16c], writes=[T_ps[2], T_ps[3]])
        for half in range(2):
            P.op("dve", lambda E, i=i, half=half, x_ap=x_ap: E.tensor_add(
                x2[:, i * D + half * 512: i * D + (half + 1) * 512],
                x_ap[:, half * 512:(half + 1) * 512],
                ps[2 + half][:, 0:512]),
                reads=[T_xld[r], T_ps[2 + half]], writes=[T_x2[i]])

    IOA = bass.IndirectOffsetOnAxis
    T_xn2 = [Tile(f"xn2fm{i}").inherit([T_th, T_y16b] + T_xnf) for i in range(nrt)]
    T_m12t = Tile("m12t").inherit([T_txb])
    T_sid = Tile("sid").inherit([T_z16])
    T_m1a = Tile("m1all").inherit([T_tdl])
    T_m2a = Tile("m2all").inherit([T_tdl])
    T_wmx = Tile("wmx")
    T_m12b = Tile("m12b")
    T_d12f = Tile("d12f")
    T_dei = [Tile("d1i"), Tile("d2i")]
    T_rsc = [Tile("rsc0").inherit([T_xdt]), Tile("rsc1").inherit([T_xdt])]

    def emit_router(i):
        s = i % 2
        base = s * 48
        top8 = rsc[:, base:base + 8]
        rw = rsc[:, base + 8:base + 8 + NE]
        nmax = rsc[:, base + 40:base + 41]
        ssum = rsc[:, base + 41:base + 42]

        def mmr(E, i=i):
            last = None
            for ck in range(KC):
                last = E.matmul(ps[2][:, 0:NE],
                                xn2fm[:, ck * toks + i * 128: ck * toks + (i + 1) * 128],
                                rwc[:, ck * NE:(ck + 1) * NE],
                                start=(ck == 0), stop=(ck == KC - 1))
            return last
        P.op("pe", mmr, reads=[T_xn2[i], T_bf16c], writes=[T_ps[2]])
        P.op("dve", lambda E, nmax=nmax: E.reduce_max(nmax, ps[2][:, 0:NE],
                                                      axis=AX.X, negate=True),
             reads=[T_ps[2]], writes=[T_rsc[s]])
        P.op("act", lambda E, rw=rw, nmax=nmax: E.activation(
            out=rw, in_=ps[2][:, 0:NE], func=ACTF.Exp, bias=nmax, scale=1.0),
            reads=[T_ps[2], T_rsc[s]], writes=[T_rsc[s]])
        P.op("dve", lambda E, rw=rw, ssum=ssum: E.reduce_sum(ssum, rw, axis=AX.X),
             reads=[T_rsc[s]], writes=[T_rsc[s]])
        P.op("dve", lambda E, nmax=nmax, ssum=ssum: E.reciprocal(nmax, ssum),
             reads=[T_rsc[s]], writes=[T_rsc[s]])
        P.op("dve", lambda E, rw=rw, nmax=nmax: E.tensor_scalar(
            out=rw, in0=rw, scalar1=nmax, scalar2=None, op0=ALU.mult),
            reads=[T_rsc[s]], writes=[T_rsc[s]])
        P.op("dve", lambda E, rw=rw, top8=top8: E.max(out=top8, in_=rw),
             reads=[T_rsc[s]], writes=[T_rsc[s]])
        P.op("dve", lambda E, i=i, top8=top8: E.tensor_copy(
            wmx[:, 2 * i:2 * i + 2], top8[:, 0:2]),
            reads=[T_rsc[s]], writes=[T_wmx])
        P.op("dve", lambda E, rw=rw, i=i, top8=top8: E.tensor_scalar(
            out=m1a[:, i * NE:(i + 1) * NE], in0=rw, scalar1=top8[:, 0:1],
            scalar2=None, op0=ALU.is_ge),
            reads=[T_rsc[s]], writes=[T_m1a])
        P.op("dve", lambda E, rw=rw, top8=top8: E.tensor_scalar(
            out=rw, in0=rw, scalar1=top8[:, 1:2], scalar2=None, op0=ALU.is_ge),
            reads=[T_rsc[s]], writes=[T_rsc[s]])
        P.op("dve", lambda E, rw=rw, i=i: E.tensor_sub(
            m2a[:, i * NE:(i + 1) * NE], rw, m1a[:, i * NE:(i + 1) * NE]),
            reads=[T_rsc[s], T_m1a], writes=[T_m2a])
        P.op("dve", lambda E, rw=rw: E.tensor_copy(m12b, rw),
             reads=[T_rsc[s]], writes=[T_m12b])
        q = i % 4
        P.op("pe", lambda E, q=q: E.matmul(
            ps[3][0:NE, q * 128:(q + 1) * 128], m12b, ident,
            start=True, stop=True),
            reads=[T_m12b, T_ident], writes=[T_ps[3]])
        if q == 3 or i == nrt - 1:
            i0 = i - q
            P.op("dve", lambda E, i0=i0, q=q: E.tensor_copy(
                m12t[:, i0 * 128:(i0 + q + 1) * 128],
                ps[3][0:NE, 0:(q + 1) * 128]),
                reads=[T_ps[3]], writes=[T_m12t])

    for g0 in range(0, nrt, 4):
        gn = min(4, nrt - g0)
        g = (g0 // 4) % 2
        for gi in range(gn):
            i = g0 + gi
            x_ap = x2[:, i * D:(i + 1) * D]
            ln_stats(x_ap, T_x2[i], g, gi, T_stats[g])
        ln_chain(g, gn, T_stats[g])
        for gi in range(gn):
            i = g0 + gi
            s = i % 2
            x_ap = x2[:, i * D:(i + 1) * D]
            xn_ap = xnb[:, s * D:(s + 1) * D]
            ln_norm(x_ap, T_x2[i], g, gi, T_stats[g], xn_ap, T_xnb[s])
            P.dma("sp", f"xnt{s}", lambda E, i=i, xn_ap=xn_ap: E.dma_start(
                out=d_xn2tm[i * 128:(i + 1) * 128, :], in_=xn_ap),
                reads=[T_xnb[s]])

            def dstfn2(half, i=i):
                return xn2fm.rearrange("p (c t) -> p c t", c=KC)[
                    :, half * 4:half * 4 + 4, i * 128:(i + 1) * 128]
            transpose_tile(xn_ap, T_xnb[s], None, None, T_xn2[i], dstfn2)

    for i in range(nrt):
        emit_router(i)


    P.op("dve", lambda E: E.tensor_tensor_scan(
        out=sid, data0=m12t, data1=m12t, initial=0.0,
        op0=ALU.add, op1=ALU.max),
        reads=[T_m12t], writes=[T_sid])
    P.op("dve", lambda E: E.tensor_scalar_min(sid, sid, float(CAP)),
         reads=[T_sid], writes=[T_sid])
    P.op("dve", lambda E: E.tensor_scalar(
        out=sid, in0=sid, scalar1=ecapc[:, 0:1], scalar2=None, op0=ALU.add),
        reads=[T_sid, T_consts], writes=[T_sid])
    for i in range(nrt):
        q = i % 4
        P.op("pe", lambda E, i=i, q=q: E.transpose(
            ps[3][:, q * NE:(q + 1) * NE], sid[:, i * 128:(i + 1) * 128],
            identf),
            reads=[T_sid, T_identf], writes=[T_ps[3]])
        if q == 3 or i == nrt - 1:
            i0 = i - q
            P.op("dve", lambda E, i0=i0, q=q: E.tensor_copy(
                wm[:, i0 * NE:(i0 + q + 1) * NE], ps[3][:, 0:(q + 1) * NE]),
                reads=[T_ps[3]], writes=[T_sidtm])
    for i in range(nrt):
        stm = wm[:, i * NE:(i + 1) * NE]
        tmp = rsc[:, 8:8 + NE]
        for r, ma, Tma in ((0, m1a, T_m1a), (1, m2a, T_m2a)):
            P.op("dve", lambda E, stm=stm, tmp=tmp, ma=ma, i=i: E.tensor_mul(
                tmp, stm, ma[:, i * NE:(i + 1) * NE]),
                reads=[T_sidtm, Tma], writes=[T_rsc[0]])
            P.op("dve", lambda E, tmp=tmp, i=i, r=r: E.reduce_sum(
                d12f[:, r * nrt + i:r * nrt + i + 1], tmp, axis=AX.X),
                reads=[T_rsc[0]], writes=[T_d12f])
    P.op("dve", lambda E: E.tensor_copy(d1i, d12f[:, 0:nrt]),
         reads=[T_d12f], writes=[T_dei[0]])
    P.op("dve", lambda E: E.tensor_copy(d2i, d12f[:, nrt:2 * nrt]),
         reads=[T_d12f], writes=[T_dei[1]])

    for s in range(2):
        P.relay("sp", f"xnt{s}")
    T_xn2tm.writer = [("E_sp", P.cnt["E_sp"])]
    for i in range(nrt):
        s = i % 2
        xn_ap = xnb[:, s * D:(s + 1) * D]
        P.dma("sp", f"xnl{s}", lambda E, i=i, xn_ap=xn_ap: E.dma_start(
            out=xn_ap, in_=d_xn2tm[i * 128:(i + 1) * 128, :]),
            reads=[T_xn2tm], writes=[T_xnb[s]])
        for r, dap in ((0, d1i), (1, d2i)):
            P.dma("gp", f"scx{2 * s + r}", lambda E, xn_ap=xn_ap, i=i, dap=dap:
                  E.indirect_dma_start(
                      out=d_xs, out_offset=IOA(dap[:, i:i + 1], 0),
                      in_=xn_ap, in_offset=None),
                  reads=[T_xnb[s], T_dei[r], T_xs])
    for sq in range(3):
        P.relay("gp", f"scx{sq}")
    scx_dep = P.relay("gp", "scx3")
    T_xs.writer = [scx_dep]

    T_xssb = [Tile("xssb0").inherit(T_xn2), Tile("xssb1").inherit(T_xn2)]
    T_xsfm = [Tile("xsfm0").inherit(T_xn2), Tile("xsfm1").inherit(T_xn2)]
    T_yst = [Tile("yst0").inherit(T_xld + [T_d12f]),
             Tile("yst1").inherit(T_xld + [T_d12f])]
    T_stgs = [[Tile("stgA").inherit([T_xc16]),
               Tile("stgB").inherit([T_z16, T_sid]),
               Tile("stgC").inherit([T_ttm]),
               Tile("stgD").inherit(T_xn2)],
              [Tile("stgE").inherit([T_m12t, T_txb, T_xzbs[0]]),
               Tile("stgF").inherit([T_m1a, T_m2a, T_tdl, T_cvbs[0]]),
               Tile("stgG").inherit([T_xdt, T_sqd, T_rsc[0], T_rsc[1]]),
               Tile("stgH").inherit([T_xld[2], T_xld[3]])]]
    stgs = [stg, stg2]

    def issue_xsl(et):
        sl = et % 2
        xss = xssb[:, sl * 2 * D:(sl + 1) * 2 * D]
        P.dma("gp", f"xsl{sl}", lambda E, et=et, xss=xss: E.dma_start(
            out=xss.rearrange("p (t d) -> p t d", t=2),
            in_=d_xs[et * CAP:(et + 1) * CAP, :].rearrange(
                "(t p) d -> p t d", p=128)),
            reads=[T_xs], writes=[T_xssb[sl]])

    def issue_stage(et):
        bank = et % 2
        for half in range(2):
            for qq in range(2):
                sq = 2 * half + qq
                P.dma("sp", f"st{bank * 4 + sq}",
                      lambda E, et=et, half=half, qq=qq, bank=bank, sq=sq:
                      E.dma_start(
                          out=stgs[bank][sq],
                          in_=d_ew[et, :, half, qq * 2048:(qq + 1) * 2048]),
                      writes=[T_stgs[bank][sq]])

    def emit_cast(et, half):
        bank = et % 2
        wsl = (et * 2 + half) % 4
        for qq in range(2):
            sq = 2 * half + qq
            if half == 0:
                P.op("dve", lambda E, wsl=wsl, qq=qq, bank=bank, sq=sq:
                     E.tensor_copy(we[wsl][:, qq * 2048:(qq + 1) * 2048],
                                   stgs[bank][sq]),
                     reads=[T_stgs[bank][sq]], writes=[T_we[wsl]])
            else:
                P.op("act", lambda E, wsl=wsl, qq=qq, bank=bank, sq=sq:
                     E.activation(out=we[wsl][:, qq * 2048:(qq + 1) * 2048],
                                  in_=stgs[bank][sq], func=ACTF.Copy),
                     reads=[T_stgs[bank][sq]], writes=[T_we[wsl]])

    def emit_trans(et):
        sl = et % 2
        g0b = 4 * sl
        xss = xssb[:, sl * 2 * D:(sl + 1) * 2 * D]
        xsf = xsfm[:, sl * 2 * D:(sl + 1) * 2 * D].rearrange(
            "p (c t) -> p c t", c=KC)
        for tau in range(2):
            for half in range(2):
                bank = g0b + 2 * tau + half

                def mmt2(E, xss=xss, tau=tau, half=half, bank=bank):
                    last = None
                    for qq in range(4):
                        c = half * 4 + qq
                        last = E.matmul(
                            ps[bank][:, qq * 128:(qq + 1) * 128],
                            xss[:, tau * D + c * 128: tau * D + (c + 1) * 128],
                            ident, start=True, stop=True)
                    return last
                P.op("pe", mmt2, reads=[T_xssb[sl], T_ident],
                     writes=[T_ps[bank]])
                if half == 0:
                    P.op("dve", lambda E, xsf=xsf, tau=tau, half=half, bank=bank:
                         E.tensor_copy(
                             xsf[:, half * 4:half * 4 + 4,
                                 tau * 128:(tau + 1) * 128],
                             ps[bank].rearrange("p (c t) -> p c t", c=4)),
                         reads=[T_ps[bank]], writes=[T_xsfm[sl]])
                else:
                    P.op("act", lambda E, xsf=xsf, tau=tau, half=half, bank=bank:
                         E.activation(
                             out=xsf[:, half * 4:half * 4 + 4,
                                     tau * 128:(tau + 1) * 128],
                             in_=ps[bank].rearrange("p (c t) -> p c t", c=4),
                             func=ACTF.Copy),
                         reads=[T_ps[bank]], writes=[T_xsfm[sl]])

    ystot = {0: 0, 1: 0}
    issue_xsl(0)
    issue_xsl(1)
    issue_stage(2)
    emit_trans(0)
    for e in range(NE):
        sl = e % 2
        g0b = 4 * sl
        xsf = xsfm[:, sl * 2 * D:(sl + 1) * 2 * D].rearrange(
            "p (c t) -> p c t", c=KC)
        ys = yst[:, sl * 2 * D:(sl + 1) * 2 * D]
        if e + 2 < NE:
            issue_xsl(e + 2)
        if e + 3 < NE:
            issue_stage(e + 3)
        if e + 1 < NE:
            emit_trans(e + 1)
        for half in range(2):
            wsl = (e * 2 + half) % 4
            for tau in range(2):
                bank = g0b + 2 * tau + half

                def mmf2(E, xsf=xsf, tau=tau, half=half, bank=bank, wsl=wsl):
                    last = None
                    for ck in range(KC):
                        last = E.matmul(
                            ps[bank][:, 0:512],
                            xsf[:, ck, tau * 128:(tau + 1) * 128],
                            we[wsl][:, ck * 512:(ck + 1) * 512],
                            start=(ck == 0), stop=(ck == KC - 1),
                            skip_group_check=True)
                    return last
                P.op("pe", mmf2, reads=[T_xsfm[sl], T_we[wsl]],
                     writes=[T_ps[bank]])
            for tau in range(2):
                bank = g0b + 2 * tau + half
                dst = ys[:, tau * D + half * 512: tau * D + (half + 1) * 512]
                if half == 0:
                    P.op("dve", lambda E, dst=dst, bank=bank: E.tensor_copy(
                        dst, ps[bank][:, 0:512]),
                        reads=[T_ps[bank]], writes=[T_yst[sl]])
                else:
                    P.op("act", lambda E, dst=dst, bank=bank: E.activation(
                        out=dst, in_=ps[bank][:, 0:512], func=ACTF.Copy),
                        reads=[T_ps[bank]], writes=[T_yst[sl]])
            if e + 2 < NE:
                emit_cast(e + 2, half)
        for tau in range(2):
            P.dma("sp", f"yst{sl}", lambda E, e=e, tau=tau, ys=ys: E.dma_start(
                out=d_ydr[e * CAP + tau * 128: e * CAP + (tau + 1) * 128, :],
                in_=ys[:, tau * D:(tau + 1) * D]),
                reads=[T_yst[sl]])
        ystot[sl] = P.cnt[f"yst{sl}"]
    T_ydr.writer = [("yst0", ystot[0]), ("yst1", ystot[1])]

    T_gb = [Tile(f"gb{sl}").inherit(T_xn2) for sl in range(4)]
    for i in range(nrt):
        for r, dap in ((0, d1i), (1, d2i)):
            sl = (2 * i + r) % 4
            gslice = gbuf[:, sl * D:(sl + 1) * D]
            P.dma("gp", f"gb{sl}", lambda E, i=i, dap=dap, gslice=gslice:
                  E.indirect_dma_start(
                      out=gslice, out_offset=None,
                      in_=d_ydr, in_offset=IOA(dap[:, i:i + 1], 0)),
                  reads=[T_ydr, T_dei[r]], writes=[T_gb[sl]])
            P.op("dve", lambda E, i=i, r=r, gslice=gslice:
                 E.scalar_tensor_tensor(
                     out=x2[:, i * D:(i + 1) * D], in0=gslice,
                     scalar=wmx[:, 2 * i + r:2 * i + r + 1],
                     in1=x2[:, i * D:(i + 1) * D],
                     op0=ALU.mult, op1=ALU.add),
                 reads=[T_gb[sl], T_wmx, T_x2[i]], writes=[T_x2[i]])

    for i in range(nrt):
        P.dma("sp", "ost", lambda E, i=i: E.dma_start(
            out=d_out[i * 128:(i + 1) * 128, :], in_=x2[:, i * D:(i + 1) * D]),
            reads=[T_x2[i]])
    P.relay("sp", "ost")

    P.emit()
    return nc


def prep_inputs(L0_attn_x, L0_attn_w_in, L0_attn_b_in, L0_attn_w_conv,
                L0_attn_b_conv, L0_attn_w_xproj, L0_attn_w_dt, L0_attn_b_dt,
                L0_attn_w_out, L0_ffn_router_w, L0_ffn_expert_weights,
                toks=TOKS):
    x = np.asarray(L0_attn_x, dtype=np.float32)
    w_in = np.asarray(L0_attn_w_in, dtype=np.float32)
    b_in = np.asarray(L0_attn_b_in, dtype=np.float32)
    w_conv = np.asarray(L0_attn_w_conv, dtype=np.float32).reshape(DI, 4)
    b_conv = np.asarray(L0_attn_b_conv, dtype=np.float32)
    w_xproj = np.asarray(L0_attn_w_xproj, dtype=np.float32)
    w_dt = np.asarray(L0_attn_w_dt, dtype=np.float32)
    b_dt = np.asarray(L0_attn_b_dt, dtype=np.float32)
    w_out = np.asarray(L0_attn_w_out, dtype=np.float32)
    router_w = np.asarray(L0_ffn_router_w, dtype=np.float32)
    expw = np.asarray(L0_ffn_expert_weights, dtype=np.float32)
    expw = np.ascontiguousarray(
        expw.reshape(NE, KC, 128, 2, 512).transpose(0, 2, 3, 1, 4)
        .reshape(NE, 128, 2, KC * 512))

    w_inT = np.ascontiguousarray(w_in[0:MW, :].T)
    b_in_pad = np.zeros((CC + 1) * 128, np.float32)
    b_in_pad[0:MW] = b_in[0:MW]
    b_in_sb = np.ascontiguousarray(b_in_pad.reshape(CC + 1, 128).T)
    w_conv_sb = np.ascontiguousarray(
        w_conv.reshape(CC, 128, 4).transpose(1, 0, 2).reshape(128, CC * 4))
    b_conv_sb = np.ascontiguousarray(b_conv.reshape(CC, 128).T)
    wxdt = np.concatenate([w_xproj, w_dt], axis=0)
    wxdt_sb = np.ascontiguousarray(
        wxdt.T.reshape(CC, 128, 48).transpose(1, 0, 2).reshape(128, CC * 48))
    w_out16T = np.ascontiguousarray(w_out[:, 0:NST].T)
    router_sb = np.ascontiguousarray(
        router_w.T.reshape(KC, 128, NE).transpose(1, 0, 2).reshape(128, KC * NE))
    b_dt_sb = np.ascontiguousarray(b_dt.reshape(NST, 1))

    shared = dict(w_inT=w_inT, b_in_sb=b_in_sb, w_conv_sb=w_conv_sb,
                  b_conv_sb=b_conv_sb, wxdt_sb=wxdt_sb, b_dt=b_dt_sb,
                  w_out16T=w_out16T, router_sb=router_sb)

    in_maps = []
    for k in range(NCORES):
        b, c = k // CPS, k % CPS
        if c == 0:
            xs = np.concatenate([np.zeros((128, D), np.float32),
                                 x[b, 0:toks]], axis=0)
            halo = np.zeros((128, 3), np.float32)
        else:
            xs = x[b, c * toks - 128: (c + 1) * toks]
            halo = np.ones((128, 3), np.float32)
        m = np.zeros((NST, NCORES), np.float32)
        for j in range((k // CPS) * CPS, k):
            m[:, j] = 1.0
        im = dict(shared)
        im["x"] = np.ascontiguousarray(xs)
        im["halo_mask"] = halo
        im["m_all"] = m
        rot = (k * 4) % NE
        im["expw"] = np.ascontiguousarray(np.roll(expw, -rot, axis=0))
        im["ecap1"] = (((np.arange(NE) - rot) % NE) * CAP - 1.0).astype(
            np.float32).reshape(NE, 1)
        in_maps.append(im)
    return in_maps


def kernel(**inputs):
    in_maps = prep_inputs(**inputs)
    nc = build_nc()
    res = run_bass_kernel_spmd(nc, in_maps, core_ids=list(range(NCORES)))
    out = np.empty((BATCH, SEQ, D), np.float32)
    for k in range(NCORES):
        b, c = k // CPS, k % CPS
        out[b, c * TOKS:(c + 1) * TOKS, :] = res.results[k]["out"]
    return out


# version 25
# speedup vs baseline: 1.4376x; 1.0068x over previous
import contextlib
import numpy as np

import concourse.bass as bass
import concourse.mybir as mybir
from concourse.bass_utils import run_bass_kernel_spmd

F32 = mybir.dt.float32
BF16 = mybir.dt.bfloat16
ALU = mybir.AluOpType
ACTF = mybir.ActivationFunctionType
AX = mybir.AxisListType

D = 1024
DI = 2048
NST = 16
BATCH, SEQ = 2, 8192
NCORES = 8
CPS = 4
TOKS = SEQ // CPS
TT = TOKS + 128
NT = TT // 128
NRT = TOKS // 128
KC = D // 128
CC = DI // 128
NE = 32
MW = DI + NST
EPS = 1e-5
CAP = 256
NSLOT = NE * CAP


class Tile:
    __slots__ = ("name", "writer", "readers")

    def __init__(self, name):
        self.name = name
        self.writer = []
        self.readers = []

    def inherit(self, parents):
        deps = []
        for p in parents:
            deps.extend(p.writer)
            deps.extend(p.readers)
        self.readers = deps
        return self


class Prog:
    ENGS = ("pe", "dve", "act", "gp", "sp")

    def __init__(self, nc):
        self.nc = nc
        self.ths = {e: [] for e in self.ENGS}
        self.ctx = contextlib.ExitStack()
        self.semh = {}
        self.cnt = {}
        self.satisfied = {e: {} for e in self.ENGS}
        self.esem = {e: "E_" + e for e in self.ENGS}

    def sem(self, name):
        if name not in self.semh:
            self.semh[name] = self.ctx.enter_context(self.nc.semaphore(name))
            self.cnt[name] = 0
        return self.semh[name]

    def _emit_wait(self, eng, sname, val):
        if val <= 0:
            return
        if self.satisfied[eng].get(sname, 0) >= val:
            return
        self.satisfied[eng][sname] = val
        h = self.sem(sname)
        self.ths[eng].append(lambda E, h=h, v=val: E.wait_ge(h, v))

    def _deps(self, eng, reads, writes):
        need = {}
        for t in reads:
            for s, v in t.writer:
                need[s] = max(need.get(s, 0), v)
        for t in writes:
            for s, v in t.writer:
                need[s] = max(need.get(s, 0), v)
            for s, v in t.readers:
                need[s] = max(need.get(s, 0), v)
        for s, v in sorted(need.items()):
            self._emit_wait(eng, s, v)

    def op(self, eng, fn, reads=(), writes=(), sem=None):
        self._deps(eng, reads, writes)
        sname = sem or self.esem[eng]
        h = self.sem(sname)
        self.ths[eng].append(lambda E, h=h, fn=fn: fn(E).then_inc(h, 1))
        self.cnt[sname] += 1
        v = self.cnt[sname]
        for t in reads:
            t.readers.append((sname, v))
        for t in writes:
            t.writer = [(sname, v)]
            t.readers = []

    def dma(self, eng, sname, fn, reads=(), writes=()):
        self._deps(eng, reads, writes)
        h = self.sem(sname)
        self.ths[eng].append(lambda E, h=h, fn=fn: fn(E).then_inc(h, 16))
        self.cnt[sname] += 16
        v = self.cnt[sname]
        for t in reads:
            t.readers.append((sname, v))
        for t in writes:
            t.writer = [(sname, v)]
            t.readers = []

    def relay(self, eng, sname):
        v = self.cnt[sname]
        self._emit_wait(eng, sname, v)
        es = self.esem[eng]
        h = self.sem(es)
        self.ths[eng].append(lambda E, h=h: E.nop().then_inc(h, 1))
        self.cnt[es] += 1
        return (es, self.cnt[es])

    def emit(self):
        nc = self.nc
        block = self.ctx.enter_context(nc.Block())
        engobj = {"pe": block.tensor, "dve": block.vector, "act": block.scalar,
                  "gp": block.gpsimd, "sp": block.sync}
        for e in self.ENGS:
            ths = self.ths[e]

            def body(E, ths=ths):
                for t in ths:
                    t(E)
            engobj[e](body)


def build_nc(toks=TOKS):
    nt = toks // 128 + 1
    nrt = toks // 128
    tt = toks + 128

    nc = bass.Bass("TRN2", target_bir_lowering=False, debug=False)
    P = Prog(nc)

    d_x = nc.dram_tensor("x", [tt, D], F32, kind="ExternalInput").ap()
    d_win = nc.dram_tensor("w_inT", [D, MW], F32, kind="ExternalInput").ap()
    d_bin = nc.dram_tensor("b_in_sb", [128, CC + 1], F32, kind="ExternalInput").ap()
    d_wc = nc.dram_tensor("w_conv_sb", [128, CC * 4], F32, kind="ExternalInput").ap()
    d_bc = nc.dram_tensor("b_conv_sb", [128, CC], F32, kind="ExternalInput").ap()
    d_wxdt = nc.dram_tensor("wxdt_sb", [128, CC * 48], F32, kind="ExternalInput").ap()
    d_bdt = nc.dram_tensor("b_dt", [NST, 1], F32, kind="ExternalInput").ap()
    d_wo = nc.dram_tensor("w_out16T", [NST, D], F32, kind="ExternalInput").ap()
    d_rw = nc.dram_tensor("router_sb", [128, KC * NE], F32, kind="ExternalInput").ap()
    d_ew = nc.dram_tensor("expw", [NE, 128, 2, KC * 512], F32, kind="ExternalInput").ap()
    d_halo = nc.dram_tensor("halo_mask", [128, 3], F32, kind="ExternalInput").ap()
    d_mall = nc.dram_tensor("m_all", [NST, NCORES], F32, kind="ExternalInput").ap()
    d_out = nc.dram_tensor("out", [toks, D], F32, kind="ExternalOutput").ap()

    d_ecap1 = nc.dram_tensor("ecap1", [NE, 1], F32, kind="ExternalInput").ap()
    d_xn2tm = nc.dram_tensor("xn2tm", [toks, D], BF16).ap()
    d_xs = nc.dram_tensor("xs", [NSLOT, D], BF16).ap()
    d_ydr = nc.dram_tensor("ydr", [NSLOT, D], BF16).ap()
    T_xn2tm = Tile("xn2tm")
    T_xs = Tile("xs")
    T_ydr = Tile("ydr")
    d_cci = nc.dram_tensor("cc_in", [2 * NST], F32).ap()
    d_cco = nc.dram_tensor("cc_out", [NCORES, 2 * NST], F32).ap()
    t_cci = Tile("cc_in")
    t_cco = Tile("cc_out")

    A = nc.alloc_sbuf_tensor
    AT = nc.alloc_sbuf_tensor_at

    def addr(h):
        return nc.lookup_mloc(h).addr

    x2_h = A("x2", [128, nrt * D], F32)
    xnfm_h = A("xnfm", [128, KC * tt], BF16)
    win_h = A("win", [128, KC * MW], BF16)
    xzb_h = A("xzb", [128, tt], F32)
    cvb_h = A("cvb", [128, tt], F32)
    xcr_h = A("xcr", [128, 2 * toks], BF16)
    xnb_h = A("xnb", [128, 2 * D], BF16)
    xld_h = A("xld", [128, 4 * D], F32)
    xc16_h = A("xc16", [NST, toks], F32)
    z16_h = A("z16", [NST, tt], F32)
    xdt_h = A("xdt48", [48, toks], F32)
    wm_h = A("wm", [128, nrt * NE], F32)
    identf_h = A("identf", [32, 32], F32)
    ecap_h2 = A("ecapc", [NE, 1], F32)
    wmx_h = A("wmx", [128, 2 * nrt], F32)
    d12f_h = A("d12f", [128, 2 * nrt], F32)
    d1i_h = A("d1i", [128, nrt], mybir.dt.int32)
    d2i_h = A("d2i", [128, nrt], mybir.dt.int32)
    m12b_h = A("m12b", [128, NE], BF16)
    st_h = A("stats", [128, 64], F32)
    bin_h = A("binc", [128, CC + 1], F32)
    wc_h = A("wcc", [128, CC * 4], F32)
    bc_h = A("bcc", [128, CC], F32)
    wxdt_h = A("wxdtc", [128, CC * 48], BF16)
    bdt_h = A("bdtc", [NST, 1], F32)
    wo_h = A("woc", [NST, D], BF16)
    rw_h = A("rwc", [128, KC * NE], BF16)
    halo_h = A("haloc", [128, 3], F32)
    mall_h = A("mallc", [NST, NCORES], F32)
    ident_h = A("identc", [128, 128], BF16)
    sm_h = A("scansm", [NST, 32], F32)
    pa_h = A("pa_all", [NST, NCORES], F32)
    he_h = A("he_all", [NST, NCORES], F32)
    pam_h = A("pam", [NST, NCORES], F32)
    hem_h = A("hem", [NST, NCORES], F32)
    st8_h = A("st8", [NST, NCORES], F32)

    if toks * 4 >= D * 4:
        sqd_h = AT("sqd", [128, D], F32, offset=addr(xdt_h))
    else:
        sqd_h = A("sqd", [128, D], F32)
    rsc_h = AT("rsc", [128, 96], F32, offset=addr(xdt_h) + 4096)
    t_h_h = AT("t_h", [NST, toks], F32, offset=addr(xnfm_h))
    y16b_h = AT("y16b", [NST, toks], BF16, offset=addr(xnfm_h) + toks * 4)
    xn2fm_h = AT("xn2fm", [128, KC * toks], BF16, offset=addr(xnfm_h))
    we_hs = [AT(f"we{q}", [128, KC * 512], BF16,
                offset=addr(win_h) + q * KC * 512 * 2) for q in range(4)]
    assert 4 * KC * 512 * 2 <= KC * MW * 2
    txb_h = AT("t_xdblB", [NST, toks], F32, offset=addr(xzb_h))
    tdl_h = AT("t_delta", [NST, toks], F32, offset=addr(cvb_h))
    xzb2_h = AT("xzb2", [128, tt], F32, offset=addr(x2_h))
    cvb2_h = AT("cvb2", [128, tt], F32, offset=addr(x2_h) + tt * 4)
    ttm_h = AT("t_tmp", [NST, toks], F32, offset=addr(xcr_h))

    m12t_h = AT("m12t", [NE, toks], F32, offset=addr(xzb_h))
    sid_h = AT("sid", [NE, toks], F32, offset=addr(z16_h))
    m1a_h = AT("m1all", [128, nrt * NE], F32, offset=addr(cvb_h))
    m2a_h = AT("m2all", [128, nrt * NE], F32,
               offset=addr(cvb_h) + nrt * NE * 4)
    assert 2 * nrt * NE * 4 <= tt * 4
    yst_h = AT("ystage", [128, 2 * 2 * D], BF16, offset=addr(xld_h))
    wreg = addr(xnfm_h)
    wsz = KC * tt * 2
    woff = [0]

    def at_w(name, shape, dtype):
        nbytes = 1
        for s_ in shape[1:]:
            nbytes *= s_
        nbytes *= (2 if dtype == BF16 else 4)
        if woff[0] + nbytes <= wsz:
            h = AT(name, shape, dtype, offset=wreg + woff[0])
            woff[0] += nbytes
        else:
            h = A(name, shape, dtype)
        return h
    xssb_h = at_w("xssb", [128, 2 * 2 * D], BF16)
    xsfm_h = at_w("xsfm", [128, 2 * 2 * D], BF16)
    gbuf_h = at_w("gbuf", [128, 4 * D], BF16)
    stg_hs = [AT("stgA", [128, 2048], F32, offset=addr(xc16_h)),
              AT("stgB", [128, 2048], F32, offset=addr(z16_h)),
              AT("stgC", [128, 2048], F32, offset=addr(xcr_h)),
              at_w("stgD", [128, 2048], F32)]
    stg2_hs = [AT("stgE", [128, 2048], F32, offset=addr(xzb_h)),
               AT("stgF", [128, 2048], F32, offset=addr(cvb_h)),
               AT("stgG", [128, 2048], F32, offset=addr(xdt_h)),
               AT("stgH", [128, 2048], F32, offset=addr(xld_h) + 8192)]
    assert woff[0] <= wsz

    rem = nc.sbuf_bytes_remaining
    assert rem >= 0, f"SBUF over budget: {rem}"

    ps = [nc.alloc_psum_tensor(f"ps{i}", [128, 512], F32).ap() for i in range(8)]

    x2 = x2_h.ap()
    xnfm = xnfm_h.ap()
    win = win_h.ap()
    xzb = xzb_h.ap()
    cvb = cvb_h.ap()
    xcr = xcr_h.ap()
    xnb = xnb_h.ap()
    xld = xld_h.ap()
    xc16 = xc16_h.ap()
    z16 = z16_h.ap()
    xdt = xdt_h.ap()
    wm = wm_h.ap()
    sqd = sqd_h.ap()
    stats = st_h.ap()
    rsc = rsc_h.ap()
    binc = bin_h.ap()
    wcc = wc_h.ap()
    bcc = bc_h.ap()
    wxdtc = wxdt_h.ap()
    bdtc = bdt_h.ap()
    woc = wo_h.ap()
    rwc = rw_h.ap()
    haloc = halo_h.ap()
    mallc = mall_h.ap()
    ident = ident_h.ap()
    sm = sm_h.ap()
    pa_all = pa_h.ap()
    he_all = he_h.ap()
    pam = pam_h.ap()
    hem = hem_h.ap()
    st8 = st8_h.ap()
    identf = identf_h.ap()
    ecapc = ecap_h2.ap()
    wmx = wmx_h.ap()
    d12f = d12f_h.ap()
    d1i = d1i_h.ap()
    d2i = d2i_h.ap()
    m12b = m12b_h.ap()
    m12t = m12t_h.ap()
    sid = sid_h.ap()
    m1a = m1a_h.ap()
    m2a = m2a_h.ap()
    yst = yst_h.ap()
    xssb = xssb_h.ap()
    xsfm = xsfm_h.ap()
    gbuf = gbuf_h.ap()
    stg = [h.ap() for h in stg_hs]
    stg2 = [h.ap() for h in stg2_hs]
    t_h = t_h_h.ap()
    y16b = y16b_h.ap()
    xn2fm = xn2fm_h.ap()
    we = [h.ap() for h in we_hs]
    txb = txb_h.ap()
    tdl = tdl_h.ap()
    ttm = ttm_h.ap()
    xzbs = [xzb, xzb2_h.ap()]
    cvbs = [cvb, cvb2_h.ap()]

    T_x2 = [Tile(f"x2_{i}") for i in range(nrt)]
    T_xnfm = Tile("xnfm")
    T_xnf = [Tile(f"xnf{j}") for j in range(nt)]
    T_win = Tile("win")
    T_xzbs = [Tile("xzb0"), Tile("xzb1")]
    T_cvbs = [Tile("cvb0"), Tile("cvb1")]
    T_xcr = [Tile("xcr0"), Tile("xcr1")]
    T_xnb = [Tile("xnb0"), Tile("xnb1")]
    T_xld = [Tile(f"xld{i}") for i in range(4)]
    T_xc16 = Tile("xc16")
    T_z16 = Tile("z16")
    T_xdt = Tile("xdt")
    T_sidtm = Tile("sidtm")
    T_sqd = Tile("sqd")
    T_stats = [Tile("stats0"), Tile("stats1")]
    T_consts = Tile("consts")
    T_ident = Tile("ident")
    T_ps = [Tile(f"ps{i}") for i in range(8)]
    T_smalls = Tile("scansmalls")
    T_pa = Tile("pa_all")
    T_he = Tile("he_all")
    T_pam = Tile("pam")
    T_hem = Tile("hem")
    T_st8 = Tile("st8")

    P.op("gp", lambda E: E.memset(ident, 0.0), writes=[T_ident])
    P.op("gp", lambda E: E.affine_select(
        out=ident, in_=ident, compare_op=ALU.not_equal, fill=1.0, base=0,
        pattern=[[-1, 128]], channel_multiplier=1),
        reads=[T_ident], writes=[T_ident])

    T_identf = Tile("identf")
    P.op("gp", lambda E: E.memset(identf, 0.0), writes=[T_identf])
    P.op("gp", lambda E: E.affine_select(
        out=identf, in_=identf, compare_op=ALU.not_equal, fill=1.0, base=0,
        pattern=[[-1, 32]], channel_multiplier=1),
        reads=[T_identf], writes=[T_identf])

    d_wci = nc.dram_tensor("wm_cci", [NST * 2], F32).ap()
    d_wco = nc.dram_tensor("wm_cco", [NCORES, NST * 2], F32).ap()
    t_wci = Tile("wm_cci")
    P.dma("gp", "wrm", lambda E: E.dma_start(out=d_wci, in_=d_mall[:, 0:2]),
          writes=[t_wci])
    P.op("gp", lambda E: E.collective_compute(
        "AllGather", ALU.bypass, replica_groups=[list(range(NCORES))],
        ins=[d_wci.opt()], outs=[d_wco.opt()]),
        reads=[t_wci], writes=[Tile("wm_cco")], sem="ccw")

    P.dma("gp", "G0", lambda E: E.dma_start(
        out=win.rearrange("p (c m) -> p c m", c=KC),
        in_=d_win.rearrange("(c p) m -> p c m", p=128)))
    P.dma("gp", "G0", lambda E: E.dma_start(out=wxdtc, in_=d_wxdt))
    P.dma("gp", "G0", lambda E: E.dma_start(out=woc, in_=d_wo))
    P.dma("gp", "G0", lambda E: E.dma_start(out=rwc, in_=d_rw))
    gdep = P.relay("gp", "G0")
    T_win.writer = [gdep]
    T_bf16c = Tile("bf16consts")
    T_bf16c.writer = [gdep]

    inv_d = 1.0 / D

    def ln_stats(x_ap, T_xsrc, g, gi, T_stat):
        col = g * 32
        P.op("dve", lambda E: E.reduce_sum(stats[:, col + gi:col + gi + 1],
                                           x_ap, axis=AX.X),
             reads=[T_xsrc], writes=[T_stat])
        P.op("act", lambda E: E.activation(out=sqd, in_=x_ap, func=ACTF.Square,
                                           accum_out=stats[:, col + 4 + gi:col + 5 + gi]),
             reads=[T_xsrc], writes=[T_sqd, T_stat])

    def ln_chain(g, n, T_stat):
        col = g * 32
        ssum = stats[:, col:col + n]
        ss2 = stats[:, col + 4:col + 4 + n]
        mean = stats[:, col + 8:col + 8 + n]
        ex2 = stats[:, col + 12:col + 12 + n]
        var = stats[:, col + 16:col + 16 + n]
        rstd = stats[:, col + 20:col + 20 + n]
        nbias = stats[:, col + 24:col + 24 + n]
        P.op("dve", lambda E: E.tensor_scalar(out=mean, in0=ssum, scalar1=inv_d,
                                              scalar2=None, op0=ALU.mult),
             reads=[T_stat], writes=[T_stat])
        P.op("dve", lambda E: E.tensor_scalar(out=ex2, in0=ss2, scalar1=inv_d,
                                              scalar2=None, op0=ALU.mult),
             reads=[T_stat], writes=[T_stat])
        P.op("dve", lambda E: E.tensor_tensor(out=var, in0=mean, in1=mean,
                                              op=ALU.mult),
             reads=[T_stat], writes=[T_stat])
        P.op("dve", lambda E: E.tensor_sub(var, var, ex2),
             reads=[T_stat], writes=[T_stat])
        P.op("dve", lambda E: E.tensor_scalar(out=var, in0=var, scalar1=EPS,
                                              scalar2=-1.0, op0=ALU.subtract,
                                              op1=ALU.mult),
             reads=[T_stat], writes=[T_stat])
        P.op("act", lambda E: E.activation(out=var, in_=var, func=ACTF.Sqrt),
             reads=[T_stat], writes=[T_stat])
        P.op("dve", lambda E: E.reciprocal(rstd, var),
             reads=[T_stat], writes=[T_stat])
        P.op("dve", lambda E: E.tensor_scalar(out=nbias, in0=mean, scalar1=-1.0,
                                              scalar2=None, op0=ALU.mult),
             reads=[T_stat], writes=[T_stat])
        P.op("dve", lambda E: E.tensor_mul(nbias, nbias, rstd),
             reads=[T_stat], writes=[T_stat])

    def ln_norm(x_ap, T_xsrc, g, gi, T_stat, xn_ap, T_xn):
        col = g * 32
        rstd = stats[:, col + 20 + gi:col + 21 + gi]
        nbias = stats[:, col + 24 + gi:col + 25 + gi]
        P.op("act", lambda E: E.activation(out=xn_ap, in_=x_ap,
                                           func=ACTF.Identity,
                                           scale=rstd, bias=nbias),
             reads=[T_xsrc, T_stat], writes=[T_xn])

    def transpose_tile(xn_ap, T_xn, dst_base_col, dst_stride, T_dst, dst_ap_fn,
                       dr=("dve", "act")):
        for half in range(2):
            pst = T_ps[half]
            psa = ps[half]

            def mmt(E, half=half, xn_ap=xn_ap, psa=psa):
                last = None
                for q in range(4):
                    c = half * 4 + q
                    last = E.matmul(psa[:, q * 128:(q + 1) * 128],
                                    xn_ap[:, c * 128:(c + 1) * 128],
                                    ident, start=True, stop=True)
                return last
            P.op("pe", mmt, reads=[T_xn, T_ident], writes=[pst])
            if dr[half] == "dve":
                P.op("dve", lambda E, half=half, psa=psa: E.tensor_copy(
                    dst_ap_fn(half), psa.rearrange("p (c t) -> p c t", c=4)),
                    reads=[pst], writes=[T_dst])
            else:
                P.op("act", lambda E, half=half, psa=psa: E.activation(
                    out=dst_ap_fn(half),
                    in_=psa.rearrange("p (c t) -> p c t", c=4),
                    func=ACTF.Copy),
                    reads=[pst], writes=[T_dst])

    for g0 in range(0, nt, 4):
        gn = min(4, nt - g0)
        g = (g0 // 4) % 2
        for gi in range(gn):
            tau = g0 + gi
            r = tau % 4
            x_ap = xld[:, r * D:(r + 1) * D]
            P.dma("sp", f"xr{r}", lambda E, x_ap=x_ap, tau=tau: E.dma_start(
                out=x_ap, in_=d_x[tau * 128:(tau + 1) * 128, :]),
                writes=[T_xld[r]])
            ln_stats(x_ap, T_xld[r], g, gi, T_stats[g])
        ln_chain(g, gn, T_stats[g])
        for gi in range(gn):
            tau = g0 + gi
            r = tau % 4
            s = tau % 2
            x_ap = xld[:, r * D:(r + 1) * D]
            xn_ap = xnb[:, s * D:(s + 1) * D]
            ln_norm(x_ap, T_xld[r], g, gi, T_stats[g], xn_ap, T_xnb[s])

            def dstfn(half, tau=tau):
                return xnfm.rearrange("p (c t) -> p c t", c=KC)[
                    :, half * 4:half * 4 + 4, tau * 128:(tau + 1) * 128]
            transpose_tile(xn_ap, T_xnb[s], None, None, T_xnf[tau], dstfn,
                           dr=("dve", "dve"))

    for dst, srcp in ((binc, d_bin), (wcc, d_wc), (bcc, d_bc), (bdtc, d_bdt),
                      (haloc, d_halo), (mallc, d_mall), (ecapc, d_ecap1)):
        P.dma("sp", "L0", lambda E, dst=dst, srcp=srcp: E.dma_start(out=dst, in_=srcp))
    P.relay("sp", "L0")
    T_consts.writer = [("E_sp", P.cnt["E_sp"])]

    tqs = []
    c0 = 0
    while c0 < tt:
        w = min(512, tt - c0)
        tqs.append((c0, w))
        c0 += w

    pend = None

    def emit_sil_xp(pend):
        mc, xc_ap, cv_ap, T_cv, s = pend
        P.op("act", lambda E, mc=mc, xc_ap=xc_ap, cv_ap=cv_ap: E.activation(
            out=xc_ap, in_=cv_ap[:, 128:tt], func=ACTF.Silu,
            bias=bcc[:, mc:mc + 1], scale=1.0),
            reads=[T_cv, T_consts], writes=[T_xcr[s]])
        if mc == 0:
            P.op("act", lambda E, cv_ap=cv_ap: E.activation(
                out=xc16, in_=cv_ap[0:NST, 128:tt], func=ACTF.Silu,
                bias=bcc[0:NST, 0:1], scale=1.0),
                reads=[T_cv, T_consts], writes=[T_xc16])

        def mmx(E, mc=mc, xc_ap=xc_ap):
            last = None
            for q in range(toks // 512):
                last = E.matmul(
                    ps[4 + q][0:48, :],
                    wxdtc.rearrange("p (c m) -> p c m", c=CC)[:, mc, :],
                    xc_ap[:, q * 512:(q + 1) * 512],
                    start=(mc == 0), stop=(mc == CC - 1),
                    skip_group_check=True)
            return last
        P.op("pe", mmx, reads=[T_bf16c, T_xcr[s]],
             writes=[T_ps[4 + q] for q in range(toks // 512)])

    for mc in range(CC + 1):
        is_z = mc == CC
        mwid = NST if is_z else 128
        moff = mc * 128
        s = mc % 2
        xz_ap = xzbs[s]
        cv_ap = cvbs[s]
        T_xz = T_xzbs[s]
        T_cv = T_cvbs[s]
        for qi, (t0, w) in enumerate(tqs):
            bank = 2 + (qi % 2)
            pst = T_ps[bank]
            psa = ps[bank]

            def mmw(E, mc=mc, mwid=mwid, moff=moff, t0=t0, w=w, psa=psa):
                last = None
                for ck in range(KC):
                    last = E.matmul(
                        psa[0:mwid, 0:w],
                        win.rearrange("p (c m) -> p c m", c=KC)[:, ck, moff:moff + mwid],
                        xnfm[:, ck * tt + t0: ck * tt + t0 + w],
                        start=(ck == 0), stop=(ck == KC - 1))
                return last
            P.op("pe", mmw, reads=[T_win] + T_xnf, writes=[pst])
            if is_z:
                P.op("act", lambda E, t0=t0, w=w, psa=psa: E.activation(
                    out=z16[:, t0:t0 + w], in_=psa[0:NST, 0:w],
                    func=ACTF.Identity,
                    bias=binc[0:NST, CC:CC + 1], scale=1.0),
                    reads=[pst, T_consts], writes=[T_z16])
            else:
                P.op("act", lambda E, t0=t0, w=w, psa=psa, mc=mc, xz_ap=xz_ap:
                     E.activation(
                         out=xz_ap[:, t0:t0 + w], in_=psa[:, 0:w],
                         func=ACTF.Identity,
                         bias=binc[:, mc:mc + 1], scale=1.0),
                     reads=[pst, T_consts], writes=[T_xz])
        if is_z:
            continue
        P.op("dve", lambda E, xz_ap=xz_ap: E.tensor_mul(
            xz_ap[:, 125:128], xz_ap[:, 125:128], haloc),
            reads=[T_xz, T_consts], writes=[T_xz])
        P.op("dve", lambda E, mc=mc, xz_ap=xz_ap, cv_ap=cv_ap: E.tensor_scalar(
            out=cv_ap[:, 125:tt], in0=xz_ap[:, 122:tt - 3],
            scalar1=wcc[:, mc * 4:mc * 4 + 1], scalar2=None, op0=ALU.mult),
            reads=[T_xz, T_consts], writes=[T_cv])
        for k in (1, 2, 3):
            P.op("dve", lambda E, mc=mc, k=k, xz_ap=xz_ap, cv_ap=cv_ap:
                 E.scalar_tensor_tensor(
                     out=cv_ap[:, 125:tt], in0=xz_ap[:, 122 + k:tt - 3 + k],
                     scalar=wcc[:, mc * 4 + k:mc * 4 + k + 1],
                     in1=cv_ap[:, 125:tt],
                     op0=ALU.mult, op1=ALU.add),
                 reads=[T_xz, T_cv, T_consts], writes=[T_cv])
        if pend is not None:
            emit_sil_xp(pend)
        pend = (mc, xcr[:, s * toks:(s + 1) * toks], cv_ap, T_cv, s)
    emit_sil_xp(pend)

    nq = toks // 512
    for q in range(nq):
        P.op("dve", lambda E, q=q: E.tensor_copy(xdt[:, q * 512:(q + 1) * 512],
                                                 ps[4 + q][0:48, :]),
             reads=[T_ps[4 + q]], writes=[T_xdt])
    T_txb = Tile("t_xdblB").inherit([T_xzbs[0]])
    T_tdl = Tile("t_delta").inherit([T_cvbs[0]])
    T_ttm = Tile("t_tmp").inherit(T_xcr)
    P.dma("sp", "sh1", lambda E: E.dma_start(out=txb, in_=xdt[16:32, :]),
          reads=[T_xdt], writes=[T_txb])
    P.dma("sp", "sh2", lambda E: E.dma_start(out=tdl, in_=xdt[32:48, :]),
          reads=[T_xdt], writes=[T_tdl])
    Aap = xdt[0:NST, :]
    P.op("act", lambda E: E.activation(out=Aap, in_=Aap, func=ACTF.Abs),
         reads=[T_xdt], writes=[T_xdt])
    P.op("act", lambda E: E.activation(out=ttm, in_=tdl, func=ACTF.Abs,
                                       bias=bdtc[:, 0:1], scale=1.0),
         reads=[T_tdl, T_consts], writes=[T_ttm])
    P.op("act", lambda E: E.activation(out=ttm, in_=ttm, func=ACTF.Exp,
                                       scale=-1.0),
         reads=[T_ttm], writes=[T_ttm])
    P.op("act", lambda E: E.activation(out=ttm, in_=ttm, func=ACTF.Ln,
                                       bias=1.0, scale=1.0),
         reads=[T_ttm], writes=[T_ttm])
    P.op("act", lambda E: E.activation(out=tdl, in_=tdl, func=ACTF.Relu,
                                       bias=bdtc[:, 0:1], scale=1.0),
         reads=[T_tdl, T_consts], writes=[T_tdl])
    P.op("dve", lambda E: E.tensor_add(tdl, tdl, ttm),
         reads=[T_tdl, T_ttm], writes=[T_tdl])
    P.op("dve", lambda E: E.tensor_mul(Aap, Aap, tdl),
         reads=[T_xdt, T_tdl], writes=[T_xdt])
    P.op("dve", lambda E: E.reduce_sum(sm[:, 0:1], Aap, axis=AX.X),
         reads=[T_xdt], writes=[T_smalls])
    P.op("act", lambda E: E.activation(out=sm[:, 0:1], in_=sm[:, 0:1],
                                       func=ACTF.Exp, scale=-1.0),
         reads=[T_smalls], writes=[T_smalls])
    P.op("act", lambda E: E.activation(out=Aap, in_=Aap, func=ACTF.Exp,
                                       scale=-1.0),
         reads=[T_xdt], writes=[T_xdt])
    P.op("dve", lambda E: E.tensor_mul(txb, txb, tdl),
         reads=[T_txb, T_tdl], writes=[T_txb])
    P.op("dve", lambda E: E.tensor_mul(txb, txb, xc16),
         reads=[T_txb, T_xc16], writes=[T_txb])
    T_th = Tile("t_h").inherit(T_xnf)
    P.op("dve", lambda E: E.tensor_tensor_scan(out=t_h, data0=Aap, data1=txb,
                                               initial=0.0, op0=ALU.mult,
                                               op1=ALU.add),
         reads=[T_xdt, T_txb], writes=[T_th])
    P.dma("sp", "cci", lambda E: E.dma_start(out=d_cci[0:NST], in_=sm[:, 0:1]),
          reads=[T_smalls], writes=[t_cci])
    P.dma("sp", "cci", lambda E: E.dma_start(out=d_cci[NST:2 * NST],
                                             in_=t_h[:, toks - 1:toks]),
          reads=[T_th], writes=[t_cci])
    P.op("gp", lambda E: E.collective_compute(
        "AllGather", ALU.bypass, replica_groups=[list(range(NCORES))],
        ins=[d_cci.opt()], outs=[d_cco.opt()]),
        reads=[t_cci], writes=[t_cco], sem="cc")
    T_we = [Tile(f"we{q}").inherit([T_win]) for q in range(4)]
    for q in range(4):
        P.dma("gp", f"we{q}", lambda E, q=q: E.dma_start(
            out=we[q], in_=d_ew[q // 2, :, q % 2, :]),
            writes=[T_we[q]])
    P.op("dve", lambda E: E.memset(xc16, 0.0),
         reads=[T_xc16], writes=[T_xc16])
    P.op("dve", lambda E: E.tensor_tensor_scan(out=ttm, data0=Aap, data1=xc16,
                                               initial=1.0, op0=ALU.mult,
                                               op1=ALU.add),
         reads=[T_xdt, T_xc16, T_ttm], writes=[T_ttm])
    P.op("dve", lambda E: E.tensor_mul(t_h, t_h, z16[:, 128:tt]),
         reads=[T_th, T_z16], writes=[T_th])
    P.op("dve", lambda E: E.tensor_mul(ttm, ttm, z16[:, 128:tt]),
         reads=[T_ttm, T_z16], writes=[T_ttm])
    P.dma("sp", "ccl", lambda E: E.dma_start(
        out=pa_all, in_=d_cco[:, 0:NST].rearrange("c s -> s c"),
        allow_slow_non_contiguous=True),
        reads=[t_cco], writes=[T_pa])
    P.dma("sp", "ccl2", lambda E: E.dma_start(
        out=he_all, in_=d_cco[:, NST:2 * NST].rearrange("c s -> s c"),
        allow_slow_non_contiguous=True),
        reads=[t_cco], writes=[T_he])
    P.op("dve", lambda E: E.tensor_scalar(out=pam, in0=pa_all, scalar1=1.0,
                                          scalar2=None, op0=ALU.subtract),
         reads=[T_pa], writes=[T_pam])
    P.op("dve", lambda E: E.tensor_mul(pam, pam, mallc),
         reads=[T_pam, T_consts], writes=[T_pam])
    P.op("dve", lambda E: E.tensor_scalar(out=pam, in0=pam, scalar1=1.0,
                                          scalar2=None, op0=ALU.add),
         reads=[T_pam], writes=[T_pam])
    P.op("dve", lambda E: E.tensor_mul(hem, he_all, mallc),
         reads=[T_he, T_consts], writes=[T_hem])
    P.op("dve", lambda E: E.tensor_tensor_scan(out=st8, data0=pam, data1=hem,
                                               initial=0.0, op0=ALU.mult,
                                               op1=ALU.add),
         reads=[T_pam, T_hem], writes=[T_st8])
    T_y16b = Tile("y16b").inherit(T_xnf)
    P.op("dve", lambda E: E.scalar_tensor_tensor(
        out=t_h, in0=ttm, scalar=st8[:, NCORES - 1:NCORES], in1=t_h,
        op0=ALU.mult, op1=ALU.add),
        reads=[T_ttm, T_st8, T_th], writes=[T_th])
    P.op("dve", lambda E: E.tensor_copy(y16b, t_h),
         reads=[T_th], writes=[T_y16b])

    for i in range(nrt):
        if i * D * 4 < 2 * tt * 4:
            T_x2[i].inherit([T_xzbs[1], T_cvbs[1]])
        tau = i + 1
        r = tau % 4
        x_ap = xld[:, r * D:(r + 1) * D]
        P.dma("sp", f"xr{r}", lambda E, x_ap=x_ap, tau=tau: E.dma_start(
            out=x_ap, in_=d_x[tau * 128:(tau + 1) * 128, :]),
            writes=[T_xld[r]])

        def mmo(E, i=i):
            E.matmul(ps[2], y16b[:, i * 128:(i + 1) * 128], woc[:, 0:512],
                     start=True, stop=True)
            return E.matmul(ps[3], y16b[:, i * 128:(i + 1) * 128],
                            woc[:, 512:1024], start=True, stop=True)
        P.op("pe", mmo, reads=[T_y16b, T_bf16c], writes=[T_ps[2], T_ps[3]])
        for half in range(2):
            P.op("dve", lambda E, i=i, half=half, x_ap=x_ap: E.tensor_add(
                x2[:, i * D + half * 512: i * D + (half + 1) * 512],
                x_ap[:, half * 512:(half + 1) * 512],
                ps[2 + half][:, 0:512]),
                reads=[T_xld[r], T_ps[2 + half]], writes=[T_x2[i]])

    IOA = bass.IndirectOffsetOnAxis
    T_xn2 = [Tile(f"xn2fm{i}").inherit([T_th, T_y16b] + T_xnf) for i in range(nrt)]
    T_m12t = Tile("m12t").inherit([T_txb])
    T_sid = Tile("sid").inherit([T_z16])
    T_m1a = Tile("m1all").inherit([T_tdl])
    T_m2a = Tile("m2all").inherit([T_tdl])
    T_wmx = Tile("wmx")
    T_m12b = Tile("m12b")
    T_d12f = Tile("d12f")
    T_dei = [Tile("d1i"), Tile("d2i")]
    T_rsc = [Tile("rsc0").inherit([T_xdt]), Tile("rsc1").inherit([T_xdt])]

    def emit_router(i):
        s = i % 2
        base = s * 48
        top8 = rsc[:, base:base + 8]
        rw = rsc[:, base + 8:base + 8 + NE]
        nmax = rsc[:, base + 40:base + 41]
        ssum = rsc[:, base + 41:base + 42]

        def mmr(E, i=i):
            last = None
            for ck in range(KC):
                last = E.matmul(ps[2][:, 0:NE],
                                xn2fm[:, ck * toks + i * 128: ck * toks + (i + 1) * 128],
                                rwc[:, ck * NE:(ck + 1) * NE],
                                start=(ck == 0), stop=(ck == KC - 1))
            return last
        P.op("pe", mmr, reads=[T_xn2[i], T_bf16c], writes=[T_ps[2]])
        P.op("dve", lambda E, nmax=nmax: E.reduce_max(nmax, ps[2][:, 0:NE],
                                                      axis=AX.X, negate=True),
             reads=[T_ps[2]], writes=[T_rsc[s]])
        P.op("act", lambda E, rw=rw, nmax=nmax, ssum=ssum: E.activation(
            out=rw, in_=ps[2][:, 0:NE], func=ACTF.Exp, bias=nmax, scale=1.0,
            accum_out=ssum),
            reads=[T_ps[2], T_rsc[s]], writes=[T_rsc[s]])
        P.op("dve", lambda E, nmax=nmax, ssum=ssum: E.reciprocal(nmax, ssum),
             reads=[T_rsc[s]], writes=[T_rsc[s]])
        P.op("dve", lambda E, rw=rw, nmax=nmax: E.tensor_scalar(
            out=rw, in0=rw, scalar1=nmax, scalar2=None, op0=ALU.mult),
            reads=[T_rsc[s]], writes=[T_rsc[s]])
        P.op("dve", lambda E, rw=rw, top8=top8: E.max(out=top8, in_=rw),
             reads=[T_rsc[s]], writes=[T_rsc[s]])
        P.op("dve", lambda E, i=i, top8=top8: E.tensor_copy(
            wmx[:, 2 * i:2 * i + 2], top8[:, 0:2]),
            reads=[T_rsc[s]], writes=[T_wmx])
        P.op("dve", lambda E, rw=rw, i=i, top8=top8: E.tensor_scalar(
            out=m1a[:, i * NE:(i + 1) * NE], in0=rw, scalar1=top8[:, 0:1],
            scalar2=None, op0=ALU.is_ge),
            reads=[T_rsc[s]], writes=[T_m1a])
        P.op("dve", lambda E, rw=rw, top8=top8: E.tensor_scalar(
            out=rw, in0=rw, scalar1=top8[:, 1:2], scalar2=None, op0=ALU.is_ge),
            reads=[T_rsc[s]], writes=[T_rsc[s]])
        P.op("dve", lambda E, rw=rw, i=i: E.tensor_sub(
            m2a[:, i * NE:(i + 1) * NE], rw, m1a[:, i * NE:(i + 1) * NE]),
            reads=[T_rsc[s], T_m1a], writes=[T_m2a])
        P.op("dve", lambda E, rw=rw: E.tensor_copy(m12b, rw),
             reads=[T_rsc[s]], writes=[T_m12b])
        q = i % 4
        P.op("pe", lambda E, q=q: E.matmul(
            ps[3][0:NE, q * 128:(q + 1) * 128], m12b, ident,
            start=True, stop=True),
            reads=[T_m12b, T_ident], writes=[T_ps[3]])
        if q == 3 or i == nrt - 1:
            i0 = i - q
            P.op("dve", lambda E, i0=i0, q=q: E.tensor_copy(
                m12t[:, i0 * 128:(i0 + q + 1) * 128],
                ps[3][0:NE, 0:(q + 1) * 128]),
                reads=[T_ps[3]], writes=[T_m12t])

    for g0 in range(0, nrt, 4):
        gn = min(4, nrt - g0)
        g = (g0 // 4) % 2
        for gi in range(gn):
            i = g0 + gi
            x_ap = x2[:, i * D:(i + 1) * D]
            ln_stats(x_ap, T_x2[i], g, gi, T_stats[g])
        ln_chain(g, gn, T_stats[g])
        for gi in range(gn):
            i = g0 + gi
            s = i % 2
            x_ap = x2[:, i * D:(i + 1) * D]
            xn_ap = xnb[:, s * D:(s + 1) * D]
            ln_norm(x_ap, T_x2[i], g, gi, T_stats[g], xn_ap, T_xnb[s])
            P.dma("sp", f"xnt{s}", lambda E, i=i, xn_ap=xn_ap: E.dma_start(
                out=d_xn2tm[i * 128:(i + 1) * 128, :], in_=xn_ap),
                reads=[T_xnb[s]])

            def dstfn2(half, i=i):
                return xn2fm.rearrange("p (c t) -> p c t", c=KC)[
                    :, half * 4:half * 4 + 4, i * 128:(i + 1) * 128]
            transpose_tile(xn_ap, T_xnb[s], None, None, T_xn2[i], dstfn2)

    for i in range(nrt):
        emit_router(i)


    P.op("dve", lambda E: E.tensor_tensor_scan(
        out=sid, data0=m12t, data1=m12t, initial=0.0,
        op0=ALU.add, op1=ALU.max),
        reads=[T_m12t], writes=[T_sid])
    P.op("dve", lambda E: E.tensor_scalar_min(sid, sid, float(CAP)),
         reads=[T_sid], writes=[T_sid])
    P.op("dve", lambda E: E.tensor_scalar(
        out=sid, in0=sid, scalar1=ecapc[:, 0:1], scalar2=None, op0=ALU.add),
        reads=[T_sid, T_consts], writes=[T_sid])
    for i in range(nrt):
        q = i % 4
        P.op("pe", lambda E, i=i, q=q: E.transpose(
            ps[3][:, q * NE:(q + 1) * NE], sid[:, i * 128:(i + 1) * 128],
            identf),
            reads=[T_sid, T_identf], writes=[T_ps[3]])
        if q == 3 or i == nrt - 1:
            i0 = i - q
            P.op("dve", lambda E, i0=i0, q=q: E.tensor_copy(
                wm[:, i0 * NE:(i0 + q + 1) * NE], ps[3][:, 0:(q + 1) * NE]),
                reads=[T_ps[3]], writes=[T_sidtm])
    for i in range(nrt):
        stm = wm[:, i * NE:(i + 1) * NE]
        tmp = rsc[:, 8:8 + NE]
        for r, ma, Tma in ((0, m1a, T_m1a), (1, m2a, T_m2a)):
            P.op("dve", lambda E, stm=stm, tmp=tmp, ma=ma, i=i: E.tensor_mul(
                tmp, stm, ma[:, i * NE:(i + 1) * NE]),
                reads=[T_sidtm, Tma], writes=[T_rsc[0]])
            P.op("dve", lambda E, tmp=tmp, i=i, r=r: E.reduce_sum(
                d12f[:, r * nrt + i:r * nrt + i + 1], tmp, axis=AX.X),
                reads=[T_rsc[0]], writes=[T_d12f])
    P.op("dve", lambda E: E.tensor_copy(d1i, d12f[:, 0:nrt]),
         reads=[T_d12f], writes=[T_dei[0]])
    P.op("dve", lambda E: E.tensor_copy(d2i, d12f[:, nrt:2 * nrt]),
         reads=[T_d12f], writes=[T_dei[1]])

    xnbx = nc.alloc_sbuf_tensor_at(
        "xnbx", [128, 2 * D], BF16, offset=addr(gbuf_h)).ap()
    T_xnbx = [Tile("xnbx0").inherit(T_xn2), Tile("xnbx1").inherit(T_xn2)]
    xnb4 = [xnb[:, 0:D], xnb[:, D:2 * D], xnbx[:, 0:D], xnbx[:, D:2 * D]]
    T_xnb4 = [T_xnb[0], T_xnb[1], T_xnbx[0], T_xnbx[1]]
    for s in range(2):
        P.relay("sp", f"xnt{s}")
    T_xn2tm.writer = [("E_sp", P.cnt["E_sp"])]
    for i in range(nrt):
        s = i % 4
        xn_ap = xnb4[s]
        P.dma("sp", f"xnl{s % 2}", lambda E, i=i, xn_ap=xn_ap: E.dma_start(
            out=xn_ap, in_=d_xn2tm[i * 128:(i + 1) * 128, :]),
            reads=[T_xn2tm], writes=[T_xnb4[s]])
        for r, dap in ((0, d1i), (1, d2i)):
            P.dma("gp", f"scx{(s % 2) * 2 + r}",
                  lambda E, xn_ap=xn_ap, i=i, dap=dap:
                  E.indirect_dma_start(
                      out=d_xs, out_offset=IOA(dap[:, i:i + 1], 0),
                      in_=xn_ap, in_offset=None),
                  reads=[T_xnb4[s], T_dei[r], T_xs])
    for sq in range(3):
        P.relay("gp", f"scx{sq}")
    scx_dep = P.relay("gp", "scx3")
    T_xs.writer = [scx_dep]

    T_xssb = [Tile("xssb0").inherit(T_xn2), Tile("xssb1").inherit(T_xn2)]
    T_xsfm = [Tile("xsfm0").inherit(T_xn2), Tile("xsfm1").inherit(T_xn2)]
    T_yst = [Tile("yst0").inherit(T_xld + [T_d12f]),
             Tile("yst1").inherit(T_xld + [T_d12f])]
    T_stgs = [[Tile("stgA").inherit([T_xc16]),
               Tile("stgB").inherit([T_z16, T_sid]),
               Tile("stgC").inherit([T_ttm]),
               Tile("stgD").inherit(T_xn2)],
              [Tile("stgE").inherit([T_m12t, T_txb, T_xzbs[0]]),
               Tile("stgF").inherit([T_m1a, T_m2a, T_tdl, T_cvbs[0]]),
               Tile("stgG").inherit([T_xdt, T_sqd, T_rsc[0], T_rsc[1]]),
               Tile("stgH").inherit([T_xld[2], T_xld[3]])]]
    stgs = [stg, stg2]

    def issue_xsl(et):
        sl = et % 2
        xss = xssb[:, sl * 2 * D:(sl + 1) * 2 * D]
        P.dma("gp", f"xsl{sl}", lambda E, et=et, xss=xss: E.dma_start(
            out=xss.rearrange("p (t d) -> p t d", t=2),
            in_=d_xs[et * CAP:(et + 1) * CAP, :].rearrange(
                "(t p) d -> p t d", p=128)),
            reads=[T_xs], writes=[T_xssb[sl]])

    def issue_stage(et):
        bank = et % 2
        for half in range(2):
            for qq in range(2):
                sq = 2 * half + qq
                P.dma("sp", f"st{bank * 4 + sq}",
                      lambda E, et=et, half=half, qq=qq, bank=bank, sq=sq:
                      E.dma_start(
                          out=stgs[bank][sq],
                          in_=d_ew[et, :, half, qq * 2048:(qq + 1) * 2048]),
                      writes=[T_stgs[bank][sq]])

    def emit_cast(et, half):
        bank = et % 2
        wsl = (et * 2 + half) % 4
        for qq in range(2):
            sq = 2 * half + qq
            if half == 0:
                P.op("dve", lambda E, wsl=wsl, qq=qq, bank=bank, sq=sq:
                     E.tensor_copy(we[wsl][:, qq * 2048:(qq + 1) * 2048],
                                   stgs[bank][sq]),
                     reads=[T_stgs[bank][sq]], writes=[T_we[wsl]])
            else:
                P.op("act", lambda E, wsl=wsl, qq=qq, bank=bank, sq=sq:
                     E.activation(out=we[wsl][:, qq * 2048:(qq + 1) * 2048],
                                  in_=stgs[bank][sq], func=ACTF.Copy),
                     reads=[T_stgs[bank][sq]], writes=[T_we[wsl]])

    def emit_trans(et):
        sl = et % 2
        g0b = 4 * sl
        xss = xssb[:, sl * 2 * D:(sl + 1) * 2 * D]
        xsf = xsfm[:, sl * 2 * D:(sl + 1) * 2 * D].rearrange(
            "p (c t) -> p c t", c=KC)
        for tau in range(2):
            for half in range(2):
                bank = g0b + 2 * tau + half

                def mmt2(E, xss=xss, tau=tau, half=half, bank=bank):
                    last = None
                    for qq in range(4):
                        c = half * 4 + qq
                        last = E.matmul(
                            ps[bank][:, qq * 128:(qq + 1) * 128],
                            xss[:, tau * D + c * 128: tau * D + (c + 1) * 128],
                            ident, start=True, stop=True)
                    return last
                P.op("pe", mmt2, reads=[T_xssb[sl], T_ident],
                     writes=[T_ps[bank]])
                if half == 0:
                    P.op("dve", lambda E, xsf=xsf, tau=tau, half=half, bank=bank:
                         E.tensor_copy(
                             xsf[:, half * 4:half * 4 + 4,
                                 tau * 128:(tau + 1) * 128],
                             ps[bank].rearrange("p (c t) -> p c t", c=4)),
                         reads=[T_ps[bank]], writes=[T_xsfm[sl]])
                else:
                    P.op("act", lambda E, xsf=xsf, tau=tau, half=half, bank=bank:
                         E.activation(
                             out=xsf[:, half * 4:half * 4 + 4,
                                     tau * 128:(tau + 1) * 128],
                             in_=ps[bank].rearrange("p (c t) -> p c t", c=4),
                             func=ACTF.Copy),
                         reads=[T_ps[bank]], writes=[T_xsfm[sl]])

    ystot = {0: 0, 1: 0}
    issue_xsl(0)
    issue_xsl(1)
    issue_stage(2)
    emit_trans(0)
    for e in range(NE):
        sl = e % 2
        g0b = 4 * sl
        xsf = xsfm[:, sl * 2 * D:(sl + 1) * 2 * D].rearrange(
            "p (c t) -> p c t", c=KC)
        ys = yst[:, sl * 2 * D:(sl + 1) * 2 * D]
        if e + 2 < NE:
            issue_xsl(e + 2)
        if e + 3 < NE:
            issue_stage(e + 3)
        if e + 1 < NE:
            emit_trans(e + 1)
        for half in range(2):
            wsl = (e * 2 + half) % 4
            for tau in range(2):
                bank = g0b + 2 * tau + half

                def mmf2(E, xsf=xsf, tau=tau, half=half, bank=bank, wsl=wsl):
                    last = None
                    for ck in range(KC):
                        last = E.matmul(
                            ps[bank][:, 0:512],
                            xsf[:, ck, tau * 128:(tau + 1) * 128],
                            we[wsl][:, ck * 512:(ck + 1) * 512],
                            start=(ck == 0), stop=(ck == KC - 1),
                            skip_group_check=True)
                    return last
                P.op("pe", mmf2, reads=[T_xsfm[sl], T_we[wsl]],
                     writes=[T_ps[bank]])
            for tau in range(2):
                bank = g0b + 2 * tau + half
                dst = ys[:, tau * D + half * 512: tau * D + (half + 1) * 512]
                if half == 0:
                    P.op("dve", lambda E, dst=dst, bank=bank: E.tensor_copy(
                        dst, ps[bank][:, 0:512]),
                        reads=[T_ps[bank]], writes=[T_yst[sl]])
                else:
                    P.op("act", lambda E, dst=dst, bank=bank: E.activation(
                        out=dst, in_=ps[bank][:, 0:512], func=ACTF.Copy),
                        reads=[T_ps[bank]], writes=[T_yst[sl]])
            if e + 2 < NE:
                emit_cast(e + 2, half)
        for tau in range(2):
            P.dma("sp", f"yst{sl}", lambda E, e=e, tau=tau, ys=ys: E.dma_start(
                out=d_ydr[e * CAP + tau * 128: e * CAP + (tau + 1) * 128, :],
                in_=ys[:, tau * D:(tau + 1) * D]),
                reads=[T_yst[sl]])
        ystot[sl] = P.cnt[f"yst{sl}"]
    T_ydr.writer = [("yst0", ystot[0]), ("yst1", ystot[1])]

    T_gb = [Tile(f"gb{sl}").inherit(T_xn2 + ([T_xnbx[sl]] if sl < 2 else []))
            for sl in range(4)]
    for i in range(nrt):
        for r, dap in ((0, d1i), (1, d2i)):
            sl = (2 * i + r) % 4
            gslice = gbuf[:, sl * D:(sl + 1) * D]
            P.dma("gp", f"gb{sl}", lambda E, i=i, dap=dap, gslice=gslice:
                  E.indirect_dma_start(
                      out=gslice, out_offset=None,
                      in_=d_ydr, in_offset=IOA(dap[:, i:i + 1], 0)),
                  reads=[T_ydr, T_dei[r]], writes=[T_gb[sl]])
            P.op("dve", lambda E, i=i, r=r, gslice=gslice:
                 E.scalar_tensor_tensor(
                     out=x2[:, i * D:(i + 1) * D], in0=gslice,
                     scalar=wmx[:, 2 * i + r:2 * i + r + 1],
                     in1=x2[:, i * D:(i + 1) * D],
                     op0=ALU.mult, op1=ALU.add),
                 reads=[T_gb[sl], T_wmx, T_x2[i]], writes=[T_x2[i]])

    for i in range(nrt):
        P.dma("sp", "ost", lambda E, i=i: E.dma_start(
            out=d_out[i * 128:(i + 1) * 128, :], in_=x2[:, i * D:(i + 1) * D]),
            reads=[T_x2[i]])
    P.relay("sp", "ost")

    P.emit()
    return nc


def prep_inputs(L0_attn_x, L0_attn_w_in, L0_attn_b_in, L0_attn_w_conv,
                L0_attn_b_conv, L0_attn_w_xproj, L0_attn_w_dt, L0_attn_b_dt,
                L0_attn_w_out, L0_ffn_router_w, L0_ffn_expert_weights,
                toks=TOKS):
    x = np.asarray(L0_attn_x, dtype=np.float32)
    w_in = np.asarray(L0_attn_w_in, dtype=np.float32)
    b_in = np.asarray(L0_attn_b_in, dtype=np.float32)
    w_conv = np.asarray(L0_attn_w_conv, dtype=np.float32).reshape(DI, 4)
    b_conv = np.asarray(L0_attn_b_conv, dtype=np.float32)
    w_xproj = np.asarray(L0_attn_w_xproj, dtype=np.float32)
    w_dt = np.asarray(L0_attn_w_dt, dtype=np.float32)
    b_dt = np.asarray(L0_attn_b_dt, dtype=np.float32)
    w_out = np.asarray(L0_attn_w_out, dtype=np.float32)
    router_w = np.asarray(L0_ffn_router_w, dtype=np.float32)
    expw = np.asarray(L0_ffn_expert_weights, dtype=np.float32)
    expw = np.ascontiguousarray(
        expw.reshape(NE, KC, 128, 2, 512).transpose(0, 2, 3, 1, 4)
        .reshape(NE, 128, 2, KC * 512))

    w_inT = np.ascontiguousarray(w_in[0:MW, :].T)
    b_in_pad = np.zeros((CC + 1) * 128, np.float32)
    b_in_pad[0:MW] = b_in[0:MW]
    b_in_sb = np.ascontiguousarray(b_in_pad.reshape(CC + 1, 128).T)
    w_conv_sb = np.ascontiguousarray(
        w_conv.reshape(CC, 128, 4).transpose(1, 0, 2).reshape(128, CC * 4))
    b_conv_sb = np.ascontiguousarray(b_conv.reshape(CC, 128).T)
    wxdt = np.concatenate([w_xproj, w_dt], axis=0)
    wxdt_sb = np.ascontiguousarray(
        wxdt.T.reshape(CC, 128, 48).transpose(1, 0, 2).reshape(128, CC * 48))
    w_out16T = np.ascontiguousarray(w_out[:, 0:NST].T)
    router_sb = np.ascontiguousarray(
        router_w.T.reshape(KC, 128, NE).transpose(1, 0, 2).reshape(128, KC * NE))
    b_dt_sb = np.ascontiguousarray(b_dt.reshape(NST, 1))

    shared = dict(w_inT=w_inT, b_in_sb=b_in_sb, w_conv_sb=w_conv_sb,
                  b_conv_sb=b_conv_sb, wxdt_sb=wxdt_sb, b_dt=b_dt_sb,
                  w_out16T=w_out16T, router_sb=router_sb)

    in_maps = []
    for k in range(NCORES):
        b, c = k // CPS, k % CPS
        if c == 0:
            xs = np.concatenate([np.zeros((128, D), np.float32),
                                 x[b, 0:toks]], axis=0)
            halo = np.zeros((128, 3), np.float32)
        else:
            xs = x[b, c * toks - 128: (c + 1) * toks]
            halo = np.ones((128, 3), np.float32)
        m = np.zeros((NST, NCORES), np.float32)
        for j in range((k // CPS) * CPS, k):
            m[:, j] = 1.0
        im = dict(shared)
        im["x"] = np.ascontiguousarray(xs)
        im["halo_mask"] = halo
        im["m_all"] = m
        rot = (k * 4) % NE
        im["expw"] = np.ascontiguousarray(np.roll(expw, -rot, axis=0))
        im["ecap1"] = (((np.arange(NE) - rot) % NE) * CAP - 1.0).astype(
            np.float32).reshape(NE, 1)
        in_maps.append(im)
    return in_maps


def kernel(**inputs):
    in_maps = prep_inputs(**inputs)
    nc = build_nc()
    res = run_bass_kernel_spmd(nc, in_maps, core_ids=list(range(NCORES)))
    out = np.empty((BATCH, SEQ, D), np.float32)
    for k in range(NCORES):
        b, c = k // CPS, k % CPS
        out[b, c * TOKS:(c + 1) * TOKS, :] = res.results[k]["out"]
    return out
